# version 32
# baseline (speedup 1.0000x reference)
import numpy as np
import ml_dtypes
from contextlib import ExitStack

import concourse.bass as bass
import concourse.mybir as mybir
from concourse.bass_utils import run_bass_kernel_spmd

F32 = mybir.dt.float32
BF16 = mybir.dt.bfloat16
AX = mybir.AxisListType
OP = mybir.AluOpType
ACTF = mybir.ActivationFunctionType

P = 128
T = 256
D = 1024
DI = 2048
DC = D // P
IC = DI // P
NT = 8
NB = 2
EPS = 1e-5
DECAY = 0.95

ENGS = ("sync", "gpsimd", "vector", "scalar", "tensor")


class Sched:
    def __init__(self):
        self.ops = {e: [] for e in ENGS}
        self.count = {e: 0 for e in ENGS}

    def emit(self, eng, fn, waits=(), inc=0):
        ws = tuple(w for w in waits if w is not None)
        self.ops[eng].append((ws, fn, inc))
        if inc:
            self.count[eng] += inc
            return (eng, self.count[eng])
        return None


def build_program(debug=False):
    nc = bass.Bass("TRN2", target_bir_lowering=False, debug=False)
    st = ExitStack()
    dr = {}

    def din(name, shape, dt=F32):
        dr[name] = nc.dram_tensor(name, list(shape), dt, kind="ExternalInput").ap()
        return dr[name]

    din("x", [NB, D, NT * T])
    din("w_in", [P, DC * 2 * DI])
    din("w_out", [P, IC * D])
    din("base_w", [P, DC * D])
    din("lora_a", [P, DC * P])
    din("lora_b", [P, D])
    din("w_x_c", [P, IC * 16])
    din("router_wt", [P, DC * 8])
    din("router_b", [8, 1])
    din("b_logt", [16, DI])
    din("conv_w", [P, IC * 4])
    din("conv_b", [P, IC])
    din("ln_g", [P, DC])
    din("ln_b", [P, DC])
    din("base_b_row", [1, D])
    din("c_t", [P, DC * NB])
    din("wmod", [D, 6 * D])
    din("ident", [P, P])
    din("eexp", [8, P], BF16)

    out_d = nc.dram_tensor("out", [NB, D, NT * T], F32, kind="ExternalOutput").ap()

    dbg = {}
    if debug:
        for nm, shp in [("d_mod", [P, 96]), ("d_vv", [P, DC * 2 * 8]),
                        ("d_h1_0", [P, DC * T]), ("d_h1_8", [P, DC * T]),
                        ("d_x1_0", [P, DC * T]), ("d_h2_0", [P, DC * T]),
                        ("d_xc_0", [P, IC * T]), ("d_wtm_0", [P, 2 * 8]),
                        ("d_resid_0", [P, DC * T]), ("d_y_0", [P, IC * T]),
                        ("d_sca1", [P, DC * T]), ("d_sca2", [P, DC * T]),
                        ("d_sd", [P, T]), ("d_md", [P, T])]:
            dbg[nm] = nc.dram_tensor(nm, shp, F32, kind="ExternalOutput").ap()

    def sb(name, shape, dt):
        return st.enter_context(nc.sbuf_tensor(name, list(shape), dt))[:]

    w_in = sb("w_in_sb", [P, DC, 2 * DI], BF16)
    w_out = sb("w_out_sb", [P, IC, D], BF16)
    base_w = sb("base_w_sb", [P, DC, D], BF16)
    lora_a = sb("lora_a_sb", [P, DC, P], BF16)
    lora_b = sb("lora_b_sb", [P, D], BF16)
    w_x_c = sb("w_x_c_sb", [P, IC, 16], BF16)
    router_w = sb("router_w_sb", [P, DC, 8], BF16)
    bT = sb("bT_sb", [16, DI], BF16)
    base_b_row = sb("base_b_row_sb", [1, D], BF16)

    conv_w = sb("conv_w_sb", [P, IC, 4], F32)
    conv_b = sb("conv_b_sb", [P, IC], F32)
    ln_g = sb("ln_g_sb", [P, DC], F32)
    ln_b = sb("ln_b_sb", [P, DC], F32)
    router_b = sb("router_b_sb", [8, 1], F32)
    ident = sb("ident_sb", [P, P], F32)
    eexp = sb("eexp_sb", [8, P], BF16)
    onesd = sb("onesd_sb", [P, P], BF16)
    ones1 = sb("ones1_sb", [1, P], BF16)
    h2ones = sb("h2ones_sb", [1, T], BF16)
    dec = sb("dec_sb", [P, T], F32)
    eps_sb = sb("eps_sb", [P, 1], F32)

    c_t = sb("c_t_sb", [P, DC, NB], F32)
    siluc = sb("siluc_sb", [P, DC, NB], BF16)
    mod_acc = sb("mod_acc_sb", [P, 96], F32)
    vv = sb("vv_sb", [P, DC, 2, 8], F32)

    big0 = sb("big0_sb", [P, 3 * D], F32)
    big0_bf = big0.bitcast(BF16)
    wm_half = [big0_bf[:, 0:3072], big0_bf[:, 3072:6144]]
    sq = big0_bf[:, 0:2048].rearrange("p (c t) -> p c t", c=DC)
    y_bf = big0_bf[:, 2048:6144].rearrange("p (c t) -> p c t", c=IC)

    x_sb = sb("x_sb", [P, 2, DC, T], BF16)
    sc_a = sb("sc_a_sb", [P, DC, T], BF16)
    resid = sb("resid_sb", [P, DC, T], BF16)
    h1 = sb("h1_sb", [P, DC, T], BF16)
    h2 = sb("h2_sb", [P, DC, T], BF16)
    x_inner = sb("x_inner_sb", [P, IC, T + 8], BF16)
    cacc = sb("cacc_sb", [P, 4, T], BF16)
    cb_sb = sb("cb_sb", [P, 3, T], BF16)
    m_bf = [sb("m_bfA", [P, T], BF16), sb("m_bfB", [P, T], BF16)]
    x_conv = sb("x_conv_sb", [P, IC, T], BF16)
    zsilu = sb("zsilu_sb", [P, 3, T], BF16)
    hema = sb("hema_sb", [P, 3, T], BF16)
    y1 = sb("y1_sb", [P, 3, T], BF16)
    s_c = sb("s_c_sb", [16, T], BF16)
    carry = sb("carry_sb", [P, IC], F32)
    out_sb = sb("out_sb", [P, DC, T], F32)

    st_mm = [sb("st_mmA", [1, T], F32), sb("st_mmB", [1, T], F32)]
    st_vpe = [sb("st_vpeA", [1, T], F32), sb("st_vpeB", [1, T], F32)]
    st_s = [sb("st_sA", [1, T], BF16), sb("st_sB", [1, T], BF16)]

    lg_fm = sb("lg_fm_sb", [8, T], F32)
    rt = sb("rt_sb", [P, 2, 8], F32)
    rm = sb("rm_sb", [P, 2, 8], F32)
    rmask = sb("rmask_sb", [P, 2, 8], F32)
    w_tm = sb("w_tm_sb", [P, 2, 8], F32)
    w_fm = sb("w_fm_sb", [8, T], BF16)
    ax_bf = sb("ax_bf_sb", [P, T], BF16)
    axs = sb("axs_sb", [P, T], BF16)

    blog_stage = out_sb.rearrange("p c t -> p (c t)")[0:16, 0:DI]

    def psum(name):
        return st.enter_context(nc.psum_tensor(name, [P, 512], F32))[:]

    pool = [psum("pool0"), psum("pool1"), psum("pool2"), psum("pool3")]
    pstat = [psum("pstatA"), psum("pstatB")]
    paux = psum("paux")
    prt = psum("prt")

    sems = {e: st.enter_context(nc.semaphore("s_" + e)) for e in ENGS}
    S = Sched()

    def dma(queue, outap, inap, waits=(), inc=16):
        return S.emit(queue, lambda e, o=outap, i=inap: e.dma_start(out=o, in_=i),
                      waits=waits, inc=inc)

    def mm(outap, lhsT, rhs, start, stop, waits=(), inc=0):
        return S.emit("tensor",
                      lambda e, o=outap, l=lhsT, r=rhs, a=start, b=stop:
                      e.matmul(o, l, r, start=a, stop=b),
                      waits=waits, inc=inc)

    def tpose(outap, inap, idap, waits=(), inc=1):
        return S.emit("tensor",
                      lambda e, o=outap, i=inap, d=idap: e.transpose(o, i, d),
                      waits=waits, inc=inc)

    def act(fn, waits=(), inc=1):
        return S.emit("scalar", fn, waits=waits, inc=inc)

    def vec(fn, waits=(), inc=1):
        return S.emit("vector", fn, waits=waits, inc=inc)

    for nm, dst in [("conv_b", conv_b), ("ln_g", ln_g), ("ln_b", ln_b),
                    ("router_b", router_b), ("ident", ident), ("eexp", eexp)]:
        dma("sync", dst, dr[nm])
    dma("sync", conv_w, dr["conv_w"].rearrange("p (c k) -> p c k", c=IC))
    t_ct = dma("sync", c_t, dr["c_t"].rearrange("p (c b) -> p c b", c=DC))
    t_blog = dma("sync", blog_stage, dr["b_logt"])
    t_sync = t_blog

    t_onesd = vec(lambda e: e.memset(onesd, 1.0 / D))
    t_ones1 = vec(lambda e: e.memset(ones1, 1.0))
    t_h2o = vec(lambda e: e.memset(h2ones, 1.0))
    t_dec = vec(lambda e: e.memset(dec, DECAY))
    t_eps = vec(lambda e: e.memset(eps_sb, EPS))

    def ka(wtok):
        S.emit("tensor",
               lambda e: e.matmul(pool[3][:, 256:384], onesd, onesd,
                                  start=True, stop=True),
               waits=[wtok], inc=0)

    last = None
    for i in range(72):
        last = mm(prt[:, 0:P], onesd, onesd, start=True, stop=True,
                  waits=([t_onesd] if i == 0 else ()),
                  inc=(1 if i == 71 else 0))
    t_warm = last

    t_siluc = act(lambda e: e.activation(siluc, c_t, ACTF.Silu), waits=[t_ct])
    t_bT = act(lambda e: e.activation(bT, blog_stage, ACTF.Exp), waits=[t_blog])

    x_view = dr["x"]

    def x_tile_ap(b, j):
        return x_view[b, :, j * T:(j + 1) * T].rearrange("(c p) t -> p c t", p=P)

    def out_tile_ap(b, j):
        return out_d[b, :, j * T:(j + 1) * T].rearrange("(c p) t -> p c t", p=P)

    t_x = [None, None]
    t_x[0] = dma("gpsimd", x_sb[:, 0], x_tile_ap(0, 0))

    t_mm_half = [None, None]
    t_macc = None
    for k in range(DC):
        for h in range(2):
            t_slab = dma("gpsimd", wm_half[h],
                         dr["wmod"][k * P:(k + 1) * P, h * 3072:(h + 1) * 3072],
                         waits=[t_mm_half[h], (t_macc if h == 0 else None)])
            last = None
            for mci in range(24):
                mc = h * 24 + mci
                w = [t_slab, t_siluc, t_warm] if mci == 0 else ()
                last = mm(prt[:, mc * NB:(mc + 1) * NB],
                          wm_half[h][:, mci * P:(mci + 1) * P], siluc[:, k, :],
                          start=True, stop=True, waits=w,
                          inc=(1 if mci == 23 else 0))
            t_mm_half[h] = last
        if k == 0:
            t_macc = vec(lambda e: e.tensor_copy(mod_acc, prt[:, 0:96]),
                         waits=[t_mm_half[0], t_mm_half[1]])
        else:
            t_macc = vec(lambda e: e.tensor_tensor(mod_acc, mod_acc,
                                                   prt[:, 0:96], OP.add),
                         waits=[t_mm_half[0], t_mm_half[1]])
    t_mod = t_macc
    t_mod_mm = t_mm_half[1]

    last = None
    for i in range(300):
        w = [t_bT] if i == 0 else ()
        last = mm(pool[0][:, 0:512], bT[0:16, 0:P], bT[0:16, 0:512],
                  start=True, stop=True, waits=w,
                  inc=(1 if i == 299 else 0))
    t_bridge = last

    t_w = {}
    t_w["w_in"] = dma("gpsimd", w_in, dr["w_in"].rearrange("p (c e) -> p c e", c=DC))
    t_w["w_out"] = dma("gpsimd", w_out, dr["w_out"].rearrange("p (c e) -> p c e", c=IC))
    t_w["base_w"] = dma("gpsimd", base_w, dr["base_w"].rearrange("p (c e) -> p c e", c=DC))
    t_w["lora_a"] = dma("gpsimd", lora_a, dr["lora_a"].rearrange("p (c e) -> p c e", c=DC))
    t_w["lora_b"] = dma("gpsimd", lora_b, dr["lora_b"])
    t_w["w_x_c"] = dma("gpsimd", w_x_c, dr["w_x_c"].rearrange("p (c e) -> p c e", c=IC))
    t_w["router"] = dma("gpsimd", router_w, dr["router_wt"].rearrange("p (c e) -> p c e", c=DC))
    t_w["base_b"] = dma("gpsimd", base_b_row, dr["base_b_row"])

    def modv(vi, c, b):
        mc = vi * DC + c
        return mod_acc[:, (mc * NB + b):(mc * NB + b + 1)]

    def emit_batch_prep(b, wait0):
        tk = wait0
        for c in range(DC):
            gav, gbv = modv(2, c, b), modv(5, c, b)
            vb = vv[:, c, b]
            if c == 0:
                tk = vec(lambda e: e.engine_nop(), waits=[tk, t_sync])
            tk = vec(lambda e, i=gav, o=vb[:, 0:1]: e.tensor_scalar(
                o, i, -1.0, None, OP.mult), waits=[tk])
            tk = vec(lambda e, i=gav, o=vb[:, 0:1]: e.tensor_tensor(
                o, o, i, OP.max), waits=[tk])
            tk = vec(lambda e, o=vb[:, 0:1]: e.tensor_scalar(
                o, o, 0.25, None, OP.max), waits=[tk])
            tk = vec(lambda e, i=gav, o=vb[:, 7:8]: e.tensor_scalar(
                o, i, 0.0, 2.0, OP.is_ge, OP.mult), waits=[tk])
            tk = vec(lambda e, o=vb[:, 7:8]: e.tensor_scalar(
                o, o, 1.0, None, OP.subtract), waits=[tk])
            tk = vec(lambda e, o=vb[:, 0:1], s=vb[:, 7:8]: e.tensor_tensor(
                o, o, s, OP.mult), waits=[tk])
            tk = vec(lambda e, o=vb[:, 1:2], a=vb[:, 0:1], g=ln_g[:, c:c + 1]:
                     e.tensor_tensor(o, a, g, OP.mult), waits=[tk])
            tk = vec(lambda e, o=vb[:, 2:3], a=vb[:, 0:1], g=ln_b[:, c:c + 1]:
                     e.tensor_tensor(o, a, g, OP.mult), waits=[tk])
            tk = vec(lambda e, i=gbv, o=vb[:, 3:4]: e.tensor_scalar(
                o, i, -1.0, None, OP.mult), waits=[tk])
            tk = vec(lambda e, i=gbv, o=vb[:, 3:4]: e.tensor_tensor(
                o, o, i, OP.max), waits=[tk])
            tk = vec(lambda e, o=vb[:, 3:4]: e.tensor_scalar(
                o, o, 0.25, None, OP.max), waits=[tk])
            tk = vec(lambda e, i=gbv, o=vb[:, 7:8]: e.tensor_scalar(
                o, i, 0.0, 2.0, OP.is_ge, OP.mult), waits=[tk])
            tk = vec(lambda e, o=vb[:, 7:8]: e.tensor_scalar(
                o, o, 1.0, None, OP.subtract), waits=[tk])
            tk = vec(lambda e, o=vb[:, 3:4], s=vb[:, 7:8]: e.tensor_tensor(
                o, o, s, OP.mult), waits=[tk])
            tk = vec(lambda e, i=modv(1, c, b), o=vb[:, 5:6]: e.tensor_scalar(
                o, i, 1.0, None, OP.add), waits=[tk])
            tk = vec(lambda e, i=modv(4, c, b), o=vb[:, 6:7]: e.tensor_scalar(
                o, i, 1.0, None, OP.add), waits=[tk])
        return tk

    bank_tok = [[t_bridge], [], [], []]
    pst_free = [[], []]
    stm_free = [None, None]
    mbf_free = [None, None]
    sbc_free = [[t_mod_mm], [t_mod_mm]]
    prt_free = [t_mod_mm, t_mod]
    misc = dict(sq_free=[t_mod_mm, t_mod], y_free=[t_mod_mm, t_mod],
                sc_a_free=None, h1_free=None, h2_free=None, resid_free=None,
                xin_free=None, xc_free=None, store=None,
                zs_free=[None, None, None], hema_free=[None, None, None],
                y1_free=[None, None, None], cacc_free=[None, None, None, None],
                cp_free=[None, None],
                cbe_free=[None, None],
                x_read=[None, None], bprep=[None, None],
                carry0=None, xin0=None)

    ln_seq = [0]

    def emit_ln_stats(src, src_toks, pe_extra=(), chunk_toks=None):
        par = ln_seq[0] % 2
        ln_seq[0] += 1
        ctok = chunk_toks if chunk_toks is not None else [None] * DC
        t_sq_c = [None] * DC
        for c in range(DC):
            w = [ctok[c] if chunk_toks is not None else None]
            if c == 0:
                w += list(src_toks) if chunk_toks is None else []
                w += misc["sq_free"]
            t_sq_c[c] = act(lambda e, c=c: e.activation(
                sq[:, c, :], src[:, c, :], ACTF.Square), waits=w)
        last = None
        for c in range(DC):
            w = [ctok[c] if chunk_toks is not None else None]
            if c == 0:
                w += list(pe_extra) + pst_free[par]
                w += list(src_toks) if chunk_toks is None else []
            last = mm(pstat[par][:, 0:T], onesd, src[:, c, :],
                      start=(c == 0), stop=(c == DC - 1), waits=w)
        for c in range(DC):
            last = mm(pstat[par][:, T:2 * T], onesd, sq[:, c, :],
                      start=(c == 0), stop=(c == DC - 1), waits=[t_sq_c[c]],
                      inc=(1 if c == DC - 1 else 0))
        t_sums = last
        misc["sq_free"] = [t_sums]
        del t_sq_c
        t_mbf = act(lambda e, par=par: e.copy(m_bf[par], pstat[par][:, 0:T]),
                    waits=[t_sums, mbf_free[par]])
        t_mmr = act(lambda e, par=par: e.activation(
            st_mm[par], pstat[par][0:1, 0:T], ACTF.Square),
            waits=[stm_free[par]])
        t_vpe = vec(lambda e, par=par: e.scalar_tensor_tensor(
            st_vpe[par], st_mm[par], -1.0, pstat[par][0:1, T:2 * T],
            OP.mult, OP.add), waits=[t_mmr])
        t_ln = act(lambda e, par=par: e.activation(
            st_vpe[par], st_vpe[par], ACTF.Ln, bias=eps_sb[0:1, 0:1]),
            waits=[t_vpe, t_eps])
        t_s = act(lambda e, par=par: e.activation(
            st_s[par], st_vpe[par], ACTF.Exp, scale=-0.5), waits=[t_ln])
        t_sbc = mm(paux[:, par * T:(par + 1) * T], ones1, st_s[par],
                   start=True, stop=True,
                   waits=[t_s, t_ones1] + sbc_free[par], inc=1)
        pst_free[par] = [t_vpe, t_mmr]
        stm_free[par] = t_vpe
        return par, t_mmr, t_sbc

    def emit_ln_apply(par, t_mbf, t_sbc, src, src_tok, gcol, acol, dst,
                      dst_free, extra0=None):
        mb3 = m_bf[par].unsqueeze(1).broadcast_to([P, DC, T])
        t_p1 = vec(lambda e, src=src, mb3=mb3: e.tensor_tensor(
            sc_a, src, mb3, OP.subtract),
            waits=[t_mbf, src_tok, misc["sc_a_free"], extra0])
        mbf_free[par] = t_p1
        toks = []
        for c in range(DC):
            tt = vec(lambda e, c=c, par=par: e.scalar_tensor_tensor(
                sc_a[:, c, :], sc_a[:, c, :], gcol(c),
                paux[:, par * T:(par + 1) * T],
                OP.mult, OP.mult), waits=[t_p1, t_sbc])
            tk = act(lambda e, c=c, dst=dst: e.activation(
                dst[:, c, :], sc_a[:, c, :], ACTF.Identity, bias=acol(c)),
                waits=[tt, (dst_free if c == 0 else None)])
            toks.append(tk)
        pst_free[par] = pst_free[par] + [t_p1]
        sbc_free[par] = [toks[-1]]
        misc["sc_a_free"] = toks[-1]
        return toks

    def emit_front(idx):
        b, j = divmod(idx, NT)
        par_t = idx % 2
        xt = x_sb[:, par_t]
        t_xl = t_x[par_t]
        pe_extra = [t_onesd] if idx == 0 else []
        par, t_mbf1, t_sbc1 = emit_ln_stats(xt, [t_xl], pe_extra)
        t_h1 = emit_ln_apply(par, t_mbf1, t_sbc1, xt, t_xl,
                             lambda c, b=b: vv[:, c, b, 5:6],
                             lambda c, b=b: modv(0, c, b),
                             h1, misc["h1_free"],
                             extra0=misc["bprep"][b])
        return dict(idx=idx, b=b, j=j, par_t=par_t, xt=xt, t_h1=t_h1)

    misc["bprep"][0] = emit_batch_prep(0, t_mod)
    misc["bprep"][1] = emit_batch_prep(1, misc["bprep"][0])
    misc["carry0"] = vec(lambda e: e.memset(carry, 0.0))
    misc["xin0"] = vec(lambda e: e.memset(x_inner[:, :, 0:4], 0.0))

    def emit_s1_pieces(fr):
        s1idx, jj = fr["idx"], fr["j"]
        t_h1a = fr["t_h1"][-1]
        out = dict(t_xconv=[None] * IC, t_mlast=None, t_ccar=None)

        def chunk(c):
            bk = c % 2
            last = None
            for k in range(DC):
                w = []
                if k == 0:
                    w = list(bank_tok[bk])
                    if c == 0:
                        w += [t_h1a, t_w["w_in"]]
                last = mm(pool[bk][:, 0:T], w_in[:, k, c * P:(c + 1) * P],
                          h1[:, k, :], start=(k == 0), stop=(k == DC - 1),
                          waits=w, inc=(1 if k == DC - 1 else 0))
            w_ev = [last]
            if c < 4:
                w_ev += [misc["xin0"] if jj == 0 else None, misc["xin_free"]]
            t_ev = act(lambda e, c=c, bk=bk: e.copy(
                x_inner[:, c, 4:4 + T], pool[bk][:, 0:T]), waits=w_ev)
            bank_tok[bk] = [t_ev]
            rp = c % 4
            tk = vec(lambda e, c=c, rp=rp: e.tensor_scalar(
                cacc[:, rp, :], x_inner[:, c, 4:4 + T], conv_w[:, c, 3:4],
                None, OP.mult),
                waits=[t_ev, misc["cacc_free"][rp],
                       (t_sync if s1idx == 0 and c == 0 else None)])
            tk = vec(lambda e, c=c, rp=rp: e.scalar_tensor_tensor(
                cacc[:, rp, :], x_inner[:, c, 3:3 + T], conv_w[:, c, 2:3],
                cacc[:, rp, :], OP.mult, OP.add), waits=[tk])
            tk = vec(lambda e, c=c, rp=rp: e.scalar_tensor_tensor(
                cacc[:, rp, :], x_inner[:, c, 2:2 + T], conv_w[:, c, 1:2],
                cacc[:, rp, :], OP.mult, OP.add), waits=[tk])
            tk = vec(lambda e, c=c, rp=rp: e.scalar_tensor_tensor(
                cacc[:, rp, :], x_inner[:, c, 1:1 + T], conv_w[:, c, 0:1],
                cacc[:, rp, :], OP.mult, OP.add), waits=[tk])
            out["t_mlast"] = tk
            t_xc = act(lambda e, c=c, rp=rp: e.activation(
                x_conv[:, c, :], cacc[:, rp, :], ACTF.Silu,
                bias=conv_b[:, c:c + 1]),
                waits=[tk, (misc["xc_free"] if c < 2 else None)])
            misc["cacc_free"][rp] = t_xc
            out["t_xconv"][c] = t_xc

        def carry_piece():
            t_ccar = vec(lambda e: e.tensor_copy(
                x_inner[:, :, 1:4], x_inner[:, :, T + 1:T + 4]),
                waits=[out["t_mlast"]])
            misc["xin_free"] = t_ccar
            out["t_ccar"] = t_ccar

        pieces = [(lambda c=c: chunk(c)) for c in range(IC)] + [carry_piece]
        return pieces, out

    front = emit_front(0)
    s1p, s1out = emit_s1_pieces(front)
    for p in s1p:
        p()

    for idx in range(NB * NT):
        b, j = front["b"], front["j"]
        xt = front["xt"]
        t_h1 = front["t_h1"]
        t_h1a = t_h1[-1]
        t_xconv = s1out["t_xconv"]
        t_ccar_cur = s1out["t_ccar"]

        nidx = idx + 1
        if nidx < NB * NT:
            nb_, nj_ = divmod(nidx, NT)
            t_x[nidx % 2] = dma("gpsimd", x_sb[:, nidx % 2],
                                x_tile_ap(nb_, nj_),
                                waits=[misc["x_read"][nidx % 2]])
        if debug and idx in (0, 8):
            dma("gpsimd", dbg["d_h1_0" if idx == 0 else "d_h1_8"],
                h1.rearrange("p c t -> p (c t)"), waits=[t_h1a])

        last = None
        for c in range(IC):
            w = [t_xconv[c]]
            if c == 0:
                w += prt_free + [t_w["w_x_c"]]
            last = mm(prt[0:16, 0:T], w_x_c[:, c, :], x_conv[:, c, :],
                      start=(c == 0), stop=(c == IC - 1), waits=w,
                      inc=(1 if c == IC - 1 else 0))
        t_sc = act(lambda e: e.copy(s_c, prt[0:16, 0:T]), waits=[last])
        prt_free = [t_sc]

        t_y = [None] * IC
        for c in range(IC):
            rp = c % 3
            cbp = c % 2
            zbk = 2 + (c % 2)
            cbs = pstat[cbp][:, 0:T]
            t_scan = vec(lambda e, c=c, rp=rp: e.tensor_tensor_scan(
                hema[:, rp, :], dec, x_conv[:, c, :], carry[:, c:c + 1],
                OP.mult, OP.add),
                waits=[t_xconv[c], misc["hema_free"][rp],
                       (misc["carry0"] if j == 0 else None), t_dec])
            vec(lambda e, c=c, rp=rp: e.tensor_copy(
                carry[:, c:c + 1], hema[:, rp, T - 1:T]), waits=[t_scan])
            t_cb = mm(cbs, bT[:, c * P:(c + 1) * P], s_c,
                      start=True, stop=True,
                      waits=[t_sc] + pst_free[cbp] +
                            [(t_bT if idx == 0 and c < 2 else None)],
                      inc=1)
            last = None
            for k in range(DC):
                w = list(bank_tok[zbk]) if k == 0 else ()
                last = mm(pool[zbk][:, 0:T],
                          w_in[:, k, (IC + c) * P:(IC + c + 1) * P],
                          h1[:, k, :], start=(k == 0), stop=(k == DC - 1),
                          waits=w, inc=(1 if k == DC - 1 else 0))
            t_z = act(lambda e, rp=rp, zb=zbk: e.activation(
                zsilu[:, rp, :], pool[zb][:, 0:T], ACTF.Silu),
                waits=[last, misc["zs_free"][rp]])
            bank_tok[zbk] = [t_z]
            t_cbe = act(lambda e, rp=rp, cb=cbs: e.copy(cb_sb[:, rp, :], cb),
                        waits=[t_cb, misc["y1_free"][rp]])
            pst_free[cbp] = [t_cbe]
            t_y1 = vec(lambda e, rp=rp: e.tensor_tensor(
                y1[:, rp, :], hema[:, rp, :], cb_sb[:, rp, :], OP.mult),
                waits=[t_scan, t_cbe, misc["y1_free"][rp]])
            misc["hema_free"][rp] = t_y1
            t_yc = vec(lambda e, c=c, rp=rp: e.tensor_tensor(
                y_bf[:, c, :], y1[:, rp, :], zsilu[:, rp, :], OP.mult),
                waits=[t_y1, t_z] + (misc["y_free"] if c < 2 else []))
            misc["y1_free"][rp] = t_yc
            misc["zs_free"][rp] = t_yc
            t_y[c] = t_yc
        misc["xc_free"] = t_y[IC - 1]

        t_r = []
        last_o = None
        for o in range(DC):
            bk = 2 + (o % 2)
            last = None
            for k in range(IC):
                w = [t_y[k]]
                if k == 0:
                    w += bank_tok[bk] + [t_w["w_out"]]
                last = mm(pool[bk][:, 0:T], w_out[:, k, o * P:(o + 1) * P],
                          y_bf[:, k, :], start=(k == 0), stop=(k == IC - 1),
                          waits=w, inc=(1 if k == IC - 1 else 0))
            last_o = last
            tk = vec(lambda e, o=o, bk=bk: e.tensor_tensor(
                resid[:, o, :], h1[:, o, :], pool[bk][:, 0:T], OP.add),
                waits=[last, (misc["resid_free"] if o == 0 else None)])
            bank_tok[bk] = [tk]
            t_r.append(tk)
        misc["h1_free"] = t_r[-1]
        misc["y_free"] = [last_o]

        wst = {}

        def ka_burst(bk, n):
            last = None
            for q in range(n):
                w = list(bank_tok[bk]) if q == 0 else ()
                last = mm(pool[bk][:, 0:512], w_in[:, 0, 0:P],
                          w_in[:, 1, 0:512], start=True, stop=True, waits=w,
                          inc=(1 if q == n - 1 else 0))
            bank_tok[bk] = [last]

        def p_attn_stats():
            t_dresid = None
            if debug and idx == 0:
                t_dresid = dma("gpsimd", dbg["d_resid_0"],
                               resid.rearrange("p c t -> p (c t)"),
                               waits=[t_r[-1]])
            parA, t_mbf2, t_sbc2 = emit_ln_stats(resid, [t_r[-1]],
                                                 chunk_toks=t_r)
            wst["parA"], wst["t_mbf2"], wst["t_sbc2"] = parA, t_mbf2, t_sbc2
            wst["t_dresid"] = t_dresid

        def p_front():
            nb_ = nidx // NT
            if nb_ != b:
                misc["carry0"] = vec(lambda e: e.memset(carry, 0.0),
                                     waits=[t_y[IC - 1]])
                misc["xin0"] = vec(lambda e: e.memset(x_inner[:, :, 0:4], 0.0),
                                   waits=[t_ccar_cur])
            wst["nfront"] = emit_front(nidx)
            wst["ns1p"], wst["ns1out"] = emit_s1_pieces(wst["nfront"])

        def p_attn_apply():
            parA, t_mbf2, t_sbc2 = wst["parA"], wst["t_mbf2"], wst["t_sbc2"]
            t_dresid = wst["t_dresid"]
            mb3 = m_bf[parA].unsqueeze(1).broadcast_to([P, DC, T])
            t_p1b = vec(lambda e, mb3=mb3: e.tensor_tensor(
                sc_a, resid, mb3, OP.subtract),
                waits=[t_mbf2, t_h1a, misc["sc_a_free"],
                       (t_dresid if debug and idx == 0 else None)])
            mbf_free[parA] = t_p1b
            t_x1 = []
            t_sdump = None
            if debug and idx == 0:
                t_cps = vec(lambda e, parA=parA: e.tensor_copy(
                    out_sb[:, 0, :], paux[:, parA * T:(parA + 1) * T]),
                    waits=[t_sbc2])
                t_cpm = vec(lambda e, pA=pstat[parA][:, 0:T]: e.tensor_copy(
                    out_sb[:, 1, :], pA), waits=[t_mbf2, t_cps])
                dma("gpsimd", dbg["d_sd"], out_sb[:, 0, :], waits=[t_cpm])
                t_sdump = dma("gpsimd", dbg["d_md"], out_sb[:, 1, :],
                              waits=[t_cpm])
            t_dsca1 = None
            if debug and idx == 0:
                t_dsca1 = dma("gpsimd", dbg["d_sca1"],
                              sc_a.rearrange("p c t -> p (c t)"),
                              waits=[t_p1b])
            tts = []
            for c in range(DC):
                tt = vec(lambda e, c=c, b=b, parA=parA: e.scalar_tensor_tensor(
                    sc_a[:, c, :], sc_a[:, c, :], vv[:, c, b, 1:2],
                    paux[:, parA * T:(parA + 1) * T],
                    OP.mult, OP.mult),
                    waits=[t_p1b, t_sbc2, (t_dsca1 if c == 0 else None),
                           (t_sdump if c == 0 and t_sdump is not None
                            else None)])
                tts.append(tt)
            t_dsca2 = None
            if debug and idx == 0:
                t_dsca2 = dma("gpsimd", dbg["d_sca2"],
                              sc_a.rearrange("p c t -> p (c t)"),
                              waits=[tts[-1]])
            for c in range(DC):
                tk = vec(lambda e, c=c, xt=xt, b=b: e.scalar_tensor_tensor(
                    resid[:, c, :], sc_a[:, c, :], vv[:, c, b, 2:3],
                    xt[:, c, :], OP.add, OP.add),
                    waits=[tts[c], (t_dsca2 if c == 0 else None)])
                t_x1.append(tk)
            pst_free[parA] = pst_free[parA] + [t_p1b]
            sbc_free[parA] = [t_x1[-1]]
            misc["sc_a_free"] = t_x1[-1]
            misc["x_read"][front["par_t"]] = t_x1[-1]
            wst["t_x1"] = t_x1

        def p_ln2():
            t_x1 = wst["t_x1"]
            parC, t_mbf3, t_sbc3 = emit_ln_stats(resid, [t_x1[-1]],
                                                 chunk_toks=t_x1)
            wst["t_h2"] = emit_ln_apply(parC, t_mbf3, t_sbc3, resid, t_x1[-1],
                                       lambda c, b=b: vv[:, c, b, 6:7],
                                       lambda c, b=b: modv(3, c, b),
                                       h2, misc["h2_free"])

        def p_rtA():
            t_h2 = wst["t_h2"]
            last = None
            for k in range(DC):
                w = [t_h2[k]]
                if k == 0:
                    w += bank_tok[2] + [t_w["router"]]
                last = mm(pool[2][0:8, 0:T], router_w[:, k, :], h2[:, k, :],
                          start=(k == 0), stop=(k == DC - 1), waits=w,
                          inc=(1 if k == DC - 1 else 0))
            t_lg = vec(lambda e: e.tensor_scalar(lg_fm, pool[2][0:8, 0:T],
                                                 router_b, None, OP.add),
                       waits=[last, (t_sync if idx == 0 else None)])
            bank_tok[2] = [t_lg]
            wst["t_lg"] = t_lg

        def p_rtB():
            t_lg = wst["t_lg"]
            t_tp = []
            for tc in range(2):
                tk = tpose(prt[:, (T + tc * 8):(T + tc * 8 + 8)],
                           lg_fm[:, tc * P:(tc + 1) * P], ident[0:8, 0:8],
                           waits=[t_lg] + (prt_free if tc == 0 else []) +
                                 [(t_sync if idx == 0 and tc == 0 else None)])
                t_tp.append(tk)
            t_rt = act(lambda e: e.copy(
                rt, prt[:, T:T + 16].rearrange("p (a e) -> p a e", a=2)),
                waits=[t_tp[-1]])
            t_wtm_pre = []
            for tc in range(2):
                tk = vec(lambda e, tc=tc: e.reduce_max(
                    rm[:, tc, 0:1], rt[:, tc, :], axis=AX.X), waits=[t_rt])
                tk = vec(lambda e, tc=tc: e.tensor_scalar(
                    rmask[:, tc, :], rt[:, tc, :], rm[:, tc, 0:1], None,
                    OP.is_lt), waits=[tk])
                tk = vec(lambda e, tc=tc: e.tensor_scalar(
                    rmask[:, tc, :], rmask[:, tc, :], 1e9, -1e9, OP.mult,
                    OP.add), waits=[tk])
                tk = vec(lambda e, tc=tc: e.tensor_tensor(
                    rmask[:, tc, :], rmask[:, tc, :], rt[:, tc, :], OP.add),
                    waits=[tk])
                tk = vec(lambda e, tc=tc: e.reduce_max(
                    rm[:, tc, 1:2], rmask[:, tc, :], axis=AX.X), waits=[tk])
                tk = vec(lambda e, tc=tc: e.tensor_tensor(
                    rm[:, tc, 2:3], rm[:, tc, 1:2], rm[:, tc, 0:1],
                    OP.subtract), waits=[tk])
                t_wtm_pre.append(tk)
            wst["t_rt"], wst["t_wtm_pre"] = t_rt, t_wtm_pre

        def p_rtC():
            t_rt, t_wtm_pre = wst["t_rt"], wst["t_wtm_pre"]
            t_w1s = act(lambda e: e.activation(rm[:, :, 5:6], rm[:, :, 2:3],
                                               ACTF.Sigmoid, scale=-1.0),
                        waits=t_wtm_pre)
            t_w2s = act(lambda e: e.activation(rm[:, :, 6:7], rm[:, :, 2:3],
                                               ACTF.Sigmoid))
            t_wtm = []
            for tc in range(2):
                tk = vec(lambda e, tc=tc: e.tensor_scalar(
                    w_tm[:, tc, :], rt[:, tc, :], rm[:, tc, 0:1], None,
                    OP.is_equal), waits=[t_w1s, t_w2s])
                tk = vec(lambda e, tc=tc: e.tensor_scalar(
                    rmask[:, tc, :], rt[:, tc, :], rm[:, tc, 1:2], None,
                    OP.is_equal), waits=[tk])
                tk = vec(lambda e, tc=tc: e.tensor_scalar(
                    rmask[:, tc, :], rmask[:, tc, :], rm[:, tc, 6:7], None,
                    OP.mult), waits=[tk])
                tk = vec(lambda e, tc=tc: e.scalar_tensor_tensor(
                    w_tm[:, tc, :], w_tm[:, tc, :], rm[:, tc, 5:6],
                    rmask[:, tc, :], OP.mult, OP.add), waits=[tk])
                t_wtm.append(tk)
            wst["t_wtm"] = t_wtm

        def p_rtD():
            t_rt, t_wtm = wst["t_rt"], wst["t_wtm"]
            t_wfm = []
            t_wfm_ev = None
            for tc in range(2):
                tk = tpose(prt[0:8, (T + 32):(T + 32 + P)],
                           w_tm[:, tc, :], ident,
                           waits=[t_wtm[tc], t_wfm_ev])
                t_wfm_ev = vec(lambda e, tc=tc: e.tensor_scalar(
                    w_fm[:, tc * P:(tc + 1) * P],
                    prt[0:8, (T + 32):(T + 32 + P)],
                    2.0, None, OP.mult), waits=[tk])
                t_wfm.append(t_wfm_ev)
            wst["t_wfm"] = t_wfm
            wst["prt_free_mid"] = [t_rt, t_wfm[-1]]

        def p_wexpax():
            t_h2 = wst["t_h2"]
            t_wfm = wst["t_wfm"]
            t_wexp = mm(prt[:, 0:T], eexp, w_fm, start=True, stop=True,
                        waits=[t_wfm[-1]] + wst["prt_free_mid"] +
                              [(t_sync if idx == 0 else None)], inc=1)
            last = None
            for k in range(DC):
                w = [t_h2[k]]
                if k == 0:
                    w += bank_tok[3] + [t_w["lora_a"]]
                last = mm(pool[3][:, 0:T], lora_a[:, k, :], h2[:, k, :],
                          start=(k == 0), stop=(k == DC - 1), waits=w,
                          inc=(1 if k == DC - 1 else 0))
            t_axe = act(lambda e: e.copy(ax_bf, pool[3][:, 0:T]),
                        waits=[last])
            bank_tok[3] = [t_axe]
            t_axs = vec(lambda e: e.tensor_tensor(axs, ax_bf, prt[:, 0:T],
                                                  OP.mult),
                        waits=[t_axe, t_wexp])
            wst["t_axs"] = t_axs
            wst["prt_free_new"] = wst["prt_free_mid"] + [t_axs]

        t_out = []

        def p_ffn(o):
            t_h2, t_axs, t_x1 = wst["t_h2"], wst["t_axs"], wst["t_x1"]
            bk = 2 + (o % 2)
            last = None
            for k in range(DC):
                w = [t_h2[k]]
                if k == 0:
                    w += bank_tok[bk] + [t_w["base_w"], t_w["lora_b"],
                                         t_w["base_b"]]
                last = mm(pool[bk][:, 0:T], base_w[:, k, o * P:(o + 1) * P],
                          h2[:, k, :], start=(k == 0), stop=False, waits=w)
            last = mm(pool[bk][:, 0:T], base_b_row[0:1, o * P:(o + 1) * P],
                      h2ones, start=False, stop=False,
                      waits=[(t_h2o if idx == 0 and o == 0 else None)])
            last = mm(pool[bk][:, 0:T], lora_b[:, o * P:(o + 1) * P], axs,
                      start=False, stop=True, waits=[t_axs], inc=1)
            tk = vec(lambda e, o=o, bk=bk, b=b: e.scalar_tensor_tensor(
                out_sb[:, o, :], pool[bk][:, 0:T], vv[:, o, b, 3:4],
                resid[:, o, :], OP.mult, OP.add),
                waits=[last, t_x1[o],
                       (misc["store"] if o == 0 else None),
                       (t_bT if idx == 0 and o == 0 else None)])
            bank_tok[bk] = [tk]
            t_out.append(tk)
            if o == DC - 1:
                misc["resid_free"] = tk
                misc["h2_free"] = last

        def p_store():
            t_store = dma("scalar", out_tile_ap(b, j), out_sb,
                          waits=[t_out[-1]])
            misc["store"] = t_store

        if nidx < NB * NT:
            p_attn_stats(); ka_burst(0, 10)
            p_front(); ka_burst(1, 10)
            ns1 = wst["ns1p"]
            p_attn_apply()
            p_ln2()
            ns1[0](); ns1[1]()
            p_rtA()
            ns1[2](); ns1[3](); ns1[4]()
            p_rtB(); p_rtC(); p_rtD()
            ns1[5](); ns1[6]()
            p_wexpax()
            ns1[7](); ns1[8](); ns1[9]()
            for o in range(DC - 1):
                p_ffn(o)
                ns1[10 + o]()
            p_ffn(DC - 1)
            p_store()
            prt_free = wst["prt_free_new"]
            front = wst["nfront"]
            s1out = wst["ns1out"]
        else:
            p_attn_stats(); p_attn_apply(); p_ln2()
            p_rtA(); p_rtB(); p_rtC(); p_rtD(); p_wexpax()
            for o in range(DC):
                p_ffn(o)
            p_store()
            prt_free = wst["prt_free_new"]

        if debug and idx == 0:
            dma("scalar", dbg["d_mod"], mod_acc, waits=[t_mod])
            dma("scalar", dbg["d_vv"],
                vv.rearrange("p c b v -> p (c b v)"),
                waits=[misc["bprep"][0]])
            dma("gpsimd", dbg["d_x1_0"], resid.rearrange("p c t -> p (c t)"),
                waits=[wst["t_x1"][-1]])
            dma("gpsimd", dbg["d_y_0"], y_bf.rearrange("p c t -> p (c t)"),
                waits=[t_y[-1]])
            dma("gpsimd", dbg["d_h2_0"], h2.rearrange("p c t -> p (c t)"),
                waits=[wst["t_h2"][-1], misc["h2_free"]])
            dma("gpsimd", dbg["d_xc_0"], x_conv.rearrange("p c t -> p (c t)"),
                waits=[t_xconv[-1], t_y[-1]])
            dma("scalar", dbg["d_wtm_0"],
                w_tm.rearrange("p a e -> p (a e)"), waits=[wst["t_wtm"][-1]])

    S.emit("scalar", lambda e: e.nop(), waits=[misc["store"]], inc=0)

    with nc.Block() as block:
        def runner(ename):
            def body(eng):
                waited = {e: 0 for e in ENGS}
                for ws, fn, inc in S.ops[ename]:
                    for (we, wv) in ws:
                        if wv <= waited[we]:
                            continue
                        eng.wait_ge(sems[we], wv)
                        waited[we] = wv
                    inst = fn(eng)
                    if inc:
                        inst.then_inc(sems[ename], inc)
            return body

        block.sync(runner("sync"))
        block.gpsimd(runner("gpsimd"))
        block.vector(runner("vector"))
        block.scalar(runner("scalar"))
        block.tensor(runner("tensor"))

    return nc, st, S


def _chunked(a, c):
    d, e = a.shape
    return np.ascontiguousarray(a.reshape(c, P, e).transpose(1, 0, 2)
                                .reshape(P, -1))


def _vec_chunk(v):
    return np.ascontiguousarray(v.reshape(-1, P).T)


def make_in_maps(L0_attn_x, L0_attn_W_in, L0_attn_conv_w, L0_attn_conv_b,
                 L0_attn_W_x, L0_attn_B_log, L0_attn_W_out, L0_attn_ln_g,
                 L0_attn_ln_b, L0_ffn_base_w, L0_ffn_base_b, L0_ffn_lora_A,
                 L0_ffn_lora_B, L0_ffn_router_w, L0_ffn_router_b, L0_c,
                 L0_wmod):
    f = lambda a: np.asarray(a, dtype=np.float32)
    x = np.ascontiguousarray(f(L0_attn_x).transpose(0, 2, 1))
    w_in = _chunked(f(L0_attn_W_in).T, DC)
    w_out = _chunked(f(L0_attn_W_out).T, IC)
    base_w = _chunked(f(L0_ffn_base_w).T, DC)
    lora_a = _chunked(f(L0_ffn_lora_A).transpose(1, 0, 2).reshape(D, P), DC)
    lora_b = np.ascontiguousarray(f(L0_ffn_lora_B).reshape(P, D))
    w_x_c = _chunked(f(L0_attn_W_x)[16:32, :].T, IC)
    router_wt = _chunked(f(L0_ffn_router_w).T, DC)
    router_b = np.ascontiguousarray(f(L0_ffn_router_b).reshape(8, 1))
    b_logt = np.ascontiguousarray(f(L0_attn_B_log).T)
    conv_w = _chunked(f(L0_attn_conv_w).reshape(DI, 4), IC)
    conv_b = _vec_chunk(f(L0_attn_conv_b))
    ln_g = _vec_chunk(f(L0_attn_ln_g))
    ln_b = _vec_chunk(f(L0_attn_ln_b))
    base_b_row = np.ascontiguousarray(f(L0_ffn_base_b).reshape(1, D))
    c_all = _chunked(f(L0_c).T, DC).reshape(P, DC, 16)
    wmod = np.ascontiguousarray(f(L0_wmod))
    ident = np.eye(P, dtype=np.float32)
    eexp = np.zeros((8, P), dtype=np.float32)
    for e in range(8):
        eexp[e, e * 16:(e + 1) * 16] = 1.0
    eexp = eexp.astype(ml_dtypes.bfloat16)

    in_maps = []
    for core in range(8):
        bsl = slice(2 * core, 2 * core + 2)
        in_maps.append({
            "x": np.ascontiguousarray(x[bsl]),
            "w_in": w_in, "w_out": w_out, "base_w": base_w,
            "lora_a": lora_a, "lora_b": lora_b, "w_x_c": w_x_c,
            "router_wt": router_wt, "router_b": router_b,
            "b_logt": b_logt, "conv_w": conv_w, "conv_b": conv_b,
            "ln_g": ln_g, "ln_b": ln_b, "base_b_row": base_b_row,
            "c_t": np.ascontiguousarray(c_all[:, :, bsl].reshape(P, -1)),
            "wmod": wmod, "ident": ident, "eexp": eexp,
        })
    return in_maps


def kernel(**inputs):
    in_maps = make_in_maps(**inputs)
    nc, stack, _ = build_program()
    try:
        res = run_bass_kernel_spmd(nc, in_maps, core_ids=list(range(8)))
    finally:
        stack.close()
    outs = []
    for core in range(8):
        o = np.asarray(res.results[core]["out"])
        outs.append(o.transpose(0, 2, 1))
    return np.ascontiguousarray(np.concatenate(outs, axis=0),
                                dtype=np.float32)


# version 33
# speedup vs baseline: 1.0334x; 1.0334x over previous
import numpy as np
import ml_dtypes
from contextlib import ExitStack

import concourse.bass as bass
import concourse.mybir as mybir
from concourse.bass_utils import run_bass_kernel_spmd

F32 = mybir.dt.float32
BF16 = mybir.dt.bfloat16
AX = mybir.AxisListType
OP = mybir.AluOpType
ACTF = mybir.ActivationFunctionType

P = 128
T = 256
D = 1024
DI = 2048
DC = D // P
IC = DI // P
NT = 8
NB = 2
EPS = 1e-5
DECAY = 0.95

ENGS = ("sync", "gpsimd", "vector", "scalar", "tensor")


class Sched:
    def __init__(self):
        self.ops = {e: [] for e in ENGS}
        self.count = {e: 0 for e in ENGS}

    def emit(self, eng, fn, waits=(), inc=0):
        ws = tuple(w for w in waits if w is not None)
        self.ops[eng].append((ws, fn, inc))
        if inc:
            self.count[eng] += inc
            return (eng, self.count[eng])
        return None


def build_program(debug=False):
    nc = bass.Bass("TRN2", target_bir_lowering=False, debug=False)
    st = ExitStack()
    dr = {}

    def din(name, shape, dt=F32):
        dr[name] = nc.dram_tensor(name, list(shape), dt, kind="ExternalInput").ap()
        return dr[name]

    din("x", [NB, D, NT * T])
    din("w_in", [P, DC * 2 * DI])
    din("w_out", [P, IC * D])
    din("base_w", [P, DC * D])
    din("lora_a", [P, DC * P])
    din("lora_b", [P, D])
    din("w_x_c", [P, IC * 16])
    din("router_wt", [P, DC * 8])
    din("router_b", [8, 1])
    din("b_logt", [16, DI])
    din("conv_w", [P, IC * 4])
    din("conv_b", [P, IC])
    din("ln_g", [P, DC])
    din("ln_b", [P, DC])
    din("base_b_row", [1, D])
    din("c_t", [P, DC * NB])
    din("wmod", [D, 6 * D])
    din("ident", [P, P])
    din("eexp", [8, P], BF16)

    out_d = nc.dram_tensor("out", [NB, D, NT * T], F32, kind="ExternalOutput").ap()

    dbg = {}
    if debug:
        for nm, shp in [("d_mod", [P, 96]), ("d_vv", [P, DC * 2 * 8]),
                        ("d_h1_0", [P, DC * T]), ("d_h1_8", [P, DC * T]),
                        ("d_x1_0", [P, DC * T]), ("d_h2_0", [P, DC * T]),
                        ("d_xc_0", [P, IC * T]), ("d_wtm_0", [P, 2 * 8]),
                        ("d_resid_0", [P, DC * T]), ("d_y_0", [P, IC * T]),
                        ("d_sca1", [P, DC * T]), ("d_sca2", [P, DC * T]),
                        ("d_sd", [P, T]), ("d_md", [P, T])]:
            dbg[nm] = nc.dram_tensor(nm, shp, F32, kind="ExternalOutput").ap()

    def sb(name, shape, dt):
        return st.enter_context(nc.sbuf_tensor(name, list(shape), dt))[:]

    w_in = sb("w_in_sb", [P, DC, 2 * DI], BF16)
    w_out = sb("w_out_sb", [P, IC, D], BF16)
    base_w = sb("base_w_sb", [P, DC, D], BF16)
    lora_a = sb("lora_a_sb", [P, DC, P], BF16)
    lora_b = sb("lora_b_sb", [P, D], BF16)
    w_x_c = sb("w_x_c_sb", [P, IC, 16], BF16)
    router_w = sb("router_w_sb", [P, DC, 8], BF16)
    bT = sb("bT_sb", [16, DI], BF16)
    base_b_row = sb("base_b_row_sb", [1, D], BF16)

    conv_w = sb("conv_w_sb", [P, IC, 4], F32)
    conv_b = sb("conv_b_sb", [P, IC], F32)
    ln_g = sb("ln_g_sb", [P, DC], F32)
    ln_b = sb("ln_b_sb", [P, DC], F32)
    router_b = sb("router_b_sb", [8, 1], F32)
    ident = sb("ident_sb", [P, P], F32)
    eexp = sb("eexp_sb", [8, P], BF16)
    onesd = sb("onesd_sb", [P, P], BF16)
    ones1 = sb("ones1_sb", [1, P], BF16)
    h2ones = sb("h2ones_sb", [1, T], BF16)
    dec = sb("dec_sb", [P, T], F32)
    eps_sb = sb("eps_sb", [P, 1], F32)

    c_t = sb("c_t_sb", [P, DC, NB], F32)
    siluc = sb("siluc_sb", [P, DC, NB], BF16)
    mod_acc = sb("mod_acc_sb", [P, 96], F32)
    vv = sb("vv_sb", [P, DC, 2, 8], F32)

    big0 = sb("big0_sb", [P, 3 * D], F32)
    big0_bf = big0.bitcast(BF16)
    wm_half = [big0_bf[:, 0:3072], big0_bf[:, 3072:6144]]
    sq = big0_bf[:, 0:2048].rearrange("p (c t) -> p c t", c=DC)
    y_bf = big0_bf[:, 2048:6144].rearrange("p (c t) -> p c t", c=IC)

    x_sb = sb("x_sb", [P, 2, DC, T], BF16)
    sc_a = sb("sc_a_sb", [P, DC, T], BF16)
    resid = sb("resid_sb", [P, DC, T], BF16)
    h1 = sb("h1_sb", [P, DC, T], BF16)
    h2 = sb("h2_sb", [P, DC, T], BF16)
    x_inner = sb("x_inner_sb", [P, IC, T + 8], BF16)
    cacc = sb("cacc_sb", [P, 4, T], BF16)
    cb_sb = sb("cb_sb", [P, 3, T], BF16)
    m_bf = [sb("m_bfA", [P, T], BF16), sb("m_bfB", [P, T], BF16)]
    x_conv = sb("x_conv_sb", [P, IC, T], BF16)
    zsilu = sb("zsilu_sb", [P, 3, T], BF16)
    hema = sb("hema_sb", [P, 3, T], BF16)
    y1 = sb("y1_sb", [P, 3, T], BF16)
    s_c = sb("s_c_sb", [16, T], BF16)
    carry = sb("carry_sb", [P, IC], F32)
    out_sb = sb("out_sb", [P, DC, T], F32)

    st_mm = [sb("st_mmA", [1, T], F32), sb("st_mmB", [1, T], F32)]
    st_vpe = [sb("st_vpeA", [1, T], F32), sb("st_vpeB", [1, T], F32)]
    st_s = [sb("st_sA", [1, T], BF16), sb("st_sB", [1, T], BF16)]

    lg_fm = sb("lg_fm_sb", [8, T], F32)
    rt = sb("rt_sb", [P, 2, 8], F32)
    rm = sb("rm_sb", [P, 2, 8], F32)
    rmask = sb("rmask_sb", [P, 2, 8], F32)
    w_tm = sb("w_tm_sb", [P, 2, 8], F32)
    w_fm = sb("w_fm_sb", [8, T], BF16)
    ax_bf = sb("ax_bf_sb", [P, T], BF16)
    axs = sb("axs_sb", [P, T], BF16)

    blog_stage = out_sb.rearrange("p c t -> p (c t)")[0:16, 0:DI]

    def psum(name):
        return st.enter_context(nc.psum_tensor(name, [P, 512], F32))[:]

    pool = [psum("pool0"), psum("pool1"), psum("pool2"), psum("pool3")]
    pstat = [psum("pstatA"), psum("pstatB")]
    paux = psum("paux")
    prt = psum("prt")

    sems = {e: st.enter_context(nc.semaphore("s_" + e)) for e in ENGS}
    S = Sched()

    def dma(queue, outap, inap, waits=(), inc=16):
        return S.emit(queue, lambda e, o=outap, i=inap: e.dma_start(out=o, in_=i),
                      waits=waits, inc=inc)

    def mm(outap, lhsT, rhs, start, stop, waits=(), inc=0):
        return S.emit("tensor",
                      lambda e, o=outap, l=lhsT, r=rhs, a=start, b=stop:
                      e.matmul(o, l, r, start=a, stop=b),
                      waits=waits, inc=inc)

    def tpose(outap, inap, idap, waits=(), inc=1):
        return S.emit("tensor",
                      lambda e, o=outap, i=inap, d=idap: e.transpose(o, i, d),
                      waits=waits, inc=inc)

    def act(fn, waits=(), inc=1):
        return S.emit("scalar", fn, waits=waits, inc=inc)

    def vec(fn, waits=(), inc=1):
        return S.emit("vector", fn, waits=waits, inc=inc)

    for nm, dst in [("conv_b", conv_b), ("ln_g", ln_g), ("ln_b", ln_b),
                    ("router_b", router_b), ("ident", ident), ("eexp", eexp)]:
        dma("sync", dst, dr[nm])
    dma("sync", conv_w, dr["conv_w"].rearrange("p (c k) -> p c k", c=IC))
    t_ct = dma("sync", c_t, dr["c_t"].rearrange("p (c b) -> p c b", c=DC))
    t_blog = dma("sync", blog_stage, dr["b_logt"])
    t_sync = t_blog

    t_onesd = vec(lambda e: e.memset(onesd, 1.0 / D))
    t_ones1 = vec(lambda e: e.memset(ones1, 1.0))
    t_h2o = vec(lambda e: e.memset(h2ones, 1.0))
    t_dec = vec(lambda e: e.memset(dec, DECAY))
    t_eps = vec(lambda e: e.memset(eps_sb, EPS))

    def ka(wtok):
        S.emit("tensor",
               lambda e: e.matmul(pool[3][:, 256:384], onesd, onesd,
                                  start=True, stop=True),
               waits=[wtok], inc=0)

    last = None
    for i in range(72):
        last = mm(prt[:, 0:P], onesd, onesd, start=True, stop=True,
                  waits=([t_onesd] if i == 0 else ()),
                  inc=(1 if i == 71 else 0))
    t_warm = last

    t_siluc = act(lambda e: e.activation(siluc, c_t, ACTF.Silu), waits=[t_ct])
    t_bT = act(lambda e: e.activation(bT, blog_stage, ACTF.Exp), waits=[t_blog])

    x_view = dr["x"]

    def x_tile_ap(b, j):
        return x_view[b, :, j * T:(j + 1) * T].rearrange("(c p) t -> p c t", p=P)

    def out_tile_ap(b, j):
        return out_d[b, :, j * T:(j + 1) * T].rearrange("(c p) t -> p c t", p=P)

    t_x = [None, None]
    t_x[0] = dma("gpsimd", x_sb[:, 0], x_tile_ap(0, 0))

    t_mm_half = [None, None]
    t_macc = None
    for k in range(DC):
        for h in range(2):
            t_slab = dma("gpsimd", wm_half[h],
                         dr["wmod"][k * P:(k + 1) * P, h * 3072:(h + 1) * 3072],
                         waits=[t_mm_half[h], (t_macc if h == 0 else None)])
            last = None
            for mci in range(24):
                mc = h * 24 + mci
                w = [t_slab, t_siluc, t_warm] if mci == 0 else ()
                last = mm(prt[:, mc * NB:(mc + 1) * NB],
                          wm_half[h][:, mci * P:(mci + 1) * P], siluc[:, k, :],
                          start=True, stop=True, waits=w,
                          inc=(1 if mci == 23 else 0))
            t_mm_half[h] = last
        if k == 0:
            t_macc = vec(lambda e: e.tensor_copy(mod_acc, prt[:, 0:96]),
                         waits=[t_mm_half[0], t_mm_half[1]])
        else:
            t_macc = vec(lambda e: e.tensor_tensor(mod_acc, mod_acc,
                                                   prt[:, 0:96], OP.add),
                         waits=[t_mm_half[0], t_mm_half[1]])
    t_mod = t_macc
    t_mod_mm = t_mm_half[1]

    t_w = {}
    t_w["w_in"] = dma("gpsimd", w_in, dr["w_in"].rearrange("p (c e) -> p c e", c=DC))
    t_w["w_out"] = dma("gpsimd", w_out, dr["w_out"].rearrange("p (c e) -> p c e", c=IC))
    t_w["base_w"] = dma("gpsimd", base_w, dr["base_w"].rearrange("p (c e) -> p c e", c=DC))
    t_w["lora_a"] = dma("gpsimd", lora_a, dr["lora_a"].rearrange("p (c e) -> p c e", c=DC))
    t_w["lora_b"] = dma("gpsimd", lora_b, dr["lora_b"])
    t_w["w_x_c"] = dma("gpsimd", w_x_c, dr["w_x_c"].rearrange("p (c e) -> p c e", c=IC))
    t_w["router"] = dma("gpsimd", router_w, dr["router_wt"].rearrange("p (c e) -> p c e", c=DC))
    t_w["base_b"] = dma("gpsimd", base_b_row, dr["base_b_row"])

    def modv(vi, c, b):
        mc = vi * DC + c
        return mod_acc[:, (mc * NB + b):(mc * NB + b + 1)]

    def emit_batch_prep(b, wait0):
        tk = wait0
        for c in range(DC):
            gav, gbv = modv(2, c, b), modv(5, c, b)
            vb = vv[:, c, b]
            if c == 0:
                tk = vec(lambda e: e.engine_nop(), waits=[tk, t_sync])
            tk = vec(lambda e, i=gav, o=vb[:, 0:1]: e.tensor_scalar(
                o, i, -1.0, None, OP.mult), waits=[tk])
            tk = vec(lambda e, i=gav, o=vb[:, 0:1]: e.tensor_tensor(
                o, o, i, OP.max), waits=[tk])
            tk = vec(lambda e, o=vb[:, 0:1]: e.tensor_scalar(
                o, o, 0.25, None, OP.max), waits=[tk])
            tk = vec(lambda e, i=gav, o=vb[:, 7:8]: e.tensor_scalar(
                o, i, 0.0, 2.0, OP.is_ge, OP.mult), waits=[tk])
            tk = vec(lambda e, o=vb[:, 7:8]: e.tensor_scalar(
                o, o, 1.0, None, OP.subtract), waits=[tk])
            tk = vec(lambda e, o=vb[:, 0:1], s=vb[:, 7:8]: e.tensor_tensor(
                o, o, s, OP.mult), waits=[tk])
            tk = vec(lambda e, o=vb[:, 1:2], a=vb[:, 0:1], g=ln_g[:, c:c + 1]:
                     e.tensor_tensor(o, a, g, OP.mult), waits=[tk])
            tk = vec(lambda e, o=vb[:, 2:3], a=vb[:, 0:1], g=ln_b[:, c:c + 1]:
                     e.tensor_tensor(o, a, g, OP.mult), waits=[tk])
            tk = vec(lambda e, i=gbv, o=vb[:, 3:4]: e.tensor_scalar(
                o, i, -1.0, None, OP.mult), waits=[tk])
            tk = vec(lambda e, i=gbv, o=vb[:, 3:4]: e.tensor_tensor(
                o, o, i, OP.max), waits=[tk])
            tk = vec(lambda e, o=vb[:, 3:4]: e.tensor_scalar(
                o, o, 0.25, None, OP.max), waits=[tk])
            tk = vec(lambda e, i=gbv, o=vb[:, 7:8]: e.tensor_scalar(
                o, i, 0.0, 2.0, OP.is_ge, OP.mult), waits=[tk])
            tk = vec(lambda e, o=vb[:, 7:8]: e.tensor_scalar(
                o, o, 1.0, None, OP.subtract), waits=[tk])
            tk = vec(lambda e, o=vb[:, 3:4], s=vb[:, 7:8]: e.tensor_tensor(
                o, o, s, OP.mult), waits=[tk])
            tk = vec(lambda e, i=modv(1, c, b), o=vb[:, 5:6]: e.tensor_scalar(
                o, i, 1.0, None, OP.add), waits=[tk])
            tk = vec(lambda e, i=modv(4, c, b), o=vb[:, 6:7]: e.tensor_scalar(
                o, i, 1.0, None, OP.add), waits=[tk])
        return tk

    bank_tok = [[], [], [], []]
    pst_free = [[], []]
    stm_free = [None, None]
    mbf_free = [None, None]
    sbc_free = [[t_mod_mm], [t_mod_mm]]
    prt_free = [t_mod_mm, t_mod]
    misc = dict(sq_free=[t_mod_mm, t_mod], y_free=[t_mod_mm, t_mod],
                sc_a_free=None, h1_free=None, h2_free=None, resid_free=None,
                xin_free=None, xc_free=None, store=[None, None],
                zs_free=[None, None, None], hema_free=[None, None, None],
                y1_free=[None, None, None], cacc_free=[None, None, None, None],
                cp_free=[None, None],
                cbe_free=[None, None],
                x_read=[None, None], bprep=[None, None],
                store2=None,
                carry0=None, xin0=None)

    ln_seq = [0]

    def emit_ln_stats(src, src_toks, pe_extra=(), chunk_toks=None):
        par = ln_seq[0] % 2
        ln_seq[0] += 1
        ctok = chunk_toks if chunk_toks is not None else [None] * DC
        t_sq_c = [None] * DC
        for c in range(0, DC, 2):
            w = [ctok[c + 1] if chunk_toks is not None else None]
            if c == 0:
                w += list(src_toks) if chunk_toks is None else []
                w += misc["sq_free"]
            tk = act(lambda e, c=c: e.activation(
                sq[:, c:c + 2, :], src[:, c:c + 2, :], ACTF.Square), waits=w)
            t_sq_c[c] = t_sq_c[c + 1] = tk
        last = None
        for c in range(DC):
            w = [ctok[c] if chunk_toks is not None else None]
            if c == 0:
                w += list(pe_extra) + pst_free[par]
                w += list(src_toks) if chunk_toks is None else []
            last = mm(pstat[par][:, 0:T], onesd, src[:, c, :],
                      start=(c == 0), stop=(c == DC - 1), waits=w)
        for c in range(DC):
            last = mm(pstat[par][:, T:2 * T], onesd, sq[:, c, :],
                      start=(c == 0), stop=(c == DC - 1), waits=[t_sq_c[c]],
                      inc=(1 if c == DC - 1 else 0))
        t_sums = last
        misc["sq_free"] = [t_sums]
        del t_sq_c
        t_mbf = act(lambda e, par=par: e.copy(m_bf[par], pstat[par][:, 0:T]),
                    waits=[t_sums, mbf_free[par]])
        t_mmr = act(lambda e, par=par: e.activation(
            st_mm[par], pstat[par][0:1, 0:T], ACTF.Square),
            waits=[stm_free[par]])
        t_vpe = vec(lambda e, par=par: e.scalar_tensor_tensor(
            st_vpe[par], st_mm[par], -1.0, pstat[par][0:1, T:2 * T],
            OP.mult, OP.add), waits=[t_mmr])
        t_ln = act(lambda e, par=par: e.activation(
            st_vpe[par], st_vpe[par], ACTF.Ln, bias=eps_sb[0:1, 0:1]),
            waits=[t_vpe, t_eps])
        t_s = act(lambda e, par=par: e.activation(
            st_s[par], st_vpe[par], ACTF.Exp, scale=-0.5), waits=[t_ln])
        t_sbc = mm(paux[:, par * T:(par + 1) * T], ones1, st_s[par],
                   start=True, stop=True,
                   waits=[t_s, t_ones1] + sbc_free[par], inc=1)
        pst_free[par] = [t_vpe, t_mmr]
        stm_free[par] = t_vpe
        return par, t_mmr, t_sbc

    def emit_ln_apply(par, t_mbf, t_sbc, src, src_tok, gcol, acol, dst,
                      dst_free, extra0=None):
        mb3 = m_bf[par].unsqueeze(1).broadcast_to([P, DC, T])
        t_p1 = vec(lambda e, src=src, mb3=mb3: e.tensor_tensor(
            sc_a, src, mb3, OP.subtract),
            waits=[t_mbf, src_tok, misc["sc_a_free"], extra0])
        mbf_free[par] = t_p1
        toks = []
        for c in range(DC):
            tt = vec(lambda e, c=c, par=par: e.scalar_tensor_tensor(
                sc_a[:, c, :], sc_a[:, c, :], gcol(c),
                paux[:, par * T:(par + 1) * T],
                OP.mult, OP.mult), waits=[t_p1, t_sbc])
            tk = act(lambda e, c=c, dst=dst: e.activation(
                dst[:, c, :], sc_a[:, c, :], ACTF.Identity, bias=acol(c)),
                waits=[tt, (dst_free if c == 0 else None)])
            toks.append(tk)
        pst_free[par] = pst_free[par] + [t_p1]
        sbc_free[par] = [toks[-1]]
        misc["sc_a_free"] = toks[-1]
        return toks

    def emit_front(idx):
        b, j = divmod(idx, NT)
        par_t = idx % 2
        xt = x_sb[:, par_t]
        t_xl = t_x[par_t]
        pe_extra = [t_onesd] if idx == 0 else []
        par, t_mbf1, t_sbc1 = emit_ln_stats(xt, [t_xl], pe_extra)
        t_h1 = emit_ln_apply(par, t_mbf1, t_sbc1, xt, t_xl,
                             lambda c, b=b: vv[:, c, b, 5:6],
                             lambda c, b=b: modv(0, c, b),
                             h1, misc["h1_free"],
                             extra0=misc["bprep"][b])
        return dict(idx=idx, b=b, j=j, par_t=par_t, xt=xt, t_h1=t_h1)

    misc["bprep"][0] = emit_batch_prep(0, t_mod)
    misc["bprep"][1] = emit_batch_prep(1, misc["bprep"][0])
    misc["carry0"] = vec(lambda e: e.memset(carry, 0.0))
    misc["xin0"] = vec(lambda e: e.memset(x_inner[:, :, 0:4], 0.0))

    def emit_s1_pieces(fr):
        s1idx, jj = fr["idx"], fr["j"]
        t_h1a = fr["t_h1"][-1]
        out = dict(t_xconv=[None] * IC, t_mlast=None, t_ccar=None)

        def chunk(c):
            bk = c % 2
            last = None
            for k in range(DC):
                w = []
                if k == 0:
                    w = list(bank_tok[bk])
                    if c == 0:
                        w += [t_h1a, t_w["w_in"]]
                last = mm(pool[bk][:, 0:T], w_in[:, k, c * P:(c + 1) * P],
                          h1[:, k, :], start=(k == 0), stop=(k == DC - 1),
                          waits=w, inc=(1 if k == DC - 1 else 0))
            w_ev = [last]
            if c < 4:
                w_ev += [misc["xin0"] if jj == 0 else None, misc["xin_free"]]
            t_ev = act(lambda e, c=c, bk=bk: e.copy(
                x_inner[:, c, 4:4 + T], pool[bk][:, 0:T]), waits=w_ev)
            bank_tok[bk] = [t_ev]
            rp = c % 4
            tk = vec(lambda e, c=c, rp=rp: e.tensor_scalar(
                cacc[:, rp, :], x_inner[:, c, 4:4 + T], conv_w[:, c, 3:4],
                None, OP.mult),
                waits=[t_ev, misc["cacc_free"][rp],
                       (t_sync if s1idx == 0 and c == 0 else None)])
            tk = vec(lambda e, c=c, rp=rp: e.scalar_tensor_tensor(
                cacc[:, rp, :], x_inner[:, c, 3:3 + T], conv_w[:, c, 2:3],
                cacc[:, rp, :], OP.mult, OP.add), waits=[tk])
            tk = vec(lambda e, c=c, rp=rp: e.scalar_tensor_tensor(
                cacc[:, rp, :], x_inner[:, c, 2:2 + T], conv_w[:, c, 1:2],
                cacc[:, rp, :], OP.mult, OP.add), waits=[tk])
            tk = vec(lambda e, c=c, rp=rp: e.scalar_tensor_tensor(
                cacc[:, rp, :], x_inner[:, c, 1:1 + T], conv_w[:, c, 0:1],
                cacc[:, rp, :], OP.mult, OP.add), waits=[tk])
            out["t_mlast"] = tk
            t_xc = act(lambda e, c=c, rp=rp: e.activation(
                x_conv[:, c, :], cacc[:, rp, :], ACTF.Silu,
                bias=conv_b[:, c:c + 1]),
                waits=[tk, (misc["xc_free"] if c < 2 else None)])
            misc["cacc_free"][rp] = t_xc
            out["t_xconv"][c] = t_xc

        def carry_piece():
            t_ccar = vec(lambda e: e.tensor_copy(
                x_inner[:, :, 1:4], x_inner[:, :, T + 1:T + 4]),
                waits=[out["t_mlast"]])
            misc["xin_free"] = t_ccar
            out["t_ccar"] = t_ccar

        pieces = [(lambda c=c: chunk(c)) for c in range(IC)] + [carry_piece]
        return pieces, out

    front = emit_front(0)
    s1p, s1out = emit_s1_pieces(front)
    for p in s1p:
        p()

    for idx in range(NB * NT):
        b, j = front["b"], front["j"]
        xt = front["xt"]
        t_h1 = front["t_h1"]
        t_h1a = t_h1[-1]
        t_xconv = s1out["t_xconv"]
        t_ccar_cur = s1out["t_ccar"]

        nidx = idx + 1
        if nidx < NB * NT:
            nb_, nj_ = divmod(nidx, NT)
            t_x[nidx % 2] = dma("gpsimd", x_sb[:, nidx % 2],
                                x_tile_ap(nb_, nj_),
                                waits=[misc["x_read"][nidx % 2]])
        if debug and idx in (0, 8):
            dma("gpsimd", dbg["d_h1_0" if idx == 0 else "d_h1_8"],
                h1.rearrange("p c t -> p (c t)"), waits=[t_h1a])

        last = None
        for c in range(IC):
            w = [t_xconv[c]]
            if c == 0:
                w += prt_free + [t_w["w_x_c"]]
            last = mm(prt[0:16, 0:T], w_x_c[:, c, :], x_conv[:, c, :],
                      start=(c == 0), stop=(c == IC - 1), waits=w,
                      inc=(1 if c == IC - 1 else 0))
        t_sc = act(lambda e: e.copy(s_c, prt[0:16, 0:T]), waits=[last])
        prt_free = [t_sc]

        t_y = [None] * IC
        for c in range(IC):
            rp = c % 3
            cbp = c % 2
            zbk = 2 + (c % 2)
            cbs = pstat[cbp][:, 0:T]
            t_scan = vec(lambda e, c=c, rp=rp: e.tensor_tensor_scan(
                hema[:, rp, :], dec, x_conv[:, c, :], carry[:, c:c + 1],
                OP.mult, OP.add),
                waits=[t_xconv[c], misc["hema_free"][rp],
                       (misc["carry0"] if j == 0 else None), t_dec])
            vec(lambda e, c=c, rp=rp: e.tensor_copy(
                carry[:, c:c + 1], hema[:, rp, T - 1:T]), waits=[t_scan])
            t_cb = mm(cbs, bT[:, c * P:(c + 1) * P], s_c,
                      start=True, stop=True,
                      waits=[t_sc] + pst_free[cbp] +
                            [(t_bT if idx == 0 and c < 2 else None)],
                      inc=1)
            last = None
            for k in range(DC):
                w = list(bank_tok[zbk]) if k == 0 else ()
                last = mm(pool[zbk][:, 0:T],
                          w_in[:, k, (IC + c) * P:(IC + c + 1) * P],
                          h1[:, k, :], start=(k == 0), stop=(k == DC - 1),
                          waits=w, inc=(1 if k == DC - 1 else 0))
            t_z = act(lambda e, rp=rp, zb=zbk: e.activation(
                zsilu[:, rp, :], pool[zb][:, 0:T], ACTF.Silu),
                waits=[last, misc["zs_free"][rp]])
            bank_tok[zbk] = [t_z]
            t_cbe = act(lambda e, rp=rp, cb=cbs: e.copy(cb_sb[:, rp, :], cb),
                        waits=[t_cb, misc["y1_free"][rp]])
            pst_free[cbp] = [t_cbe]
            t_y1 = vec(lambda e, rp=rp: e.tensor_tensor(
                y1[:, rp, :], hema[:, rp, :], cb_sb[:, rp, :], OP.mult),
                waits=[t_scan, t_cbe, misc["y1_free"][rp]])
            misc["hema_free"][rp] = t_y1
            t_yc = vec(lambda e, c=c, rp=rp: e.tensor_tensor(
                y_bf[:, c, :], y1[:, rp, :], zsilu[:, rp, :], OP.mult),
                waits=[t_y1, t_z] + (misc["y_free"] if c < 2 else []))
            misc["y1_free"][rp] = t_yc
            misc["zs_free"][rp] = t_yc
            t_y[c] = t_yc
        misc["xc_free"] = t_y[IC - 1]

        t_r = []
        last_o = None
        for o in range(DC):
            bk = 2 + (o % 2)
            last = None
            for k in range(IC):
                w = [t_y[k]]
                if k == 0:
                    w += bank_tok[bk] + [t_w["w_out"]]
                last = mm(pool[bk][:, 0:T], w_out[:, k, o * P:(o + 1) * P],
                          y_bf[:, k, :], start=(k == 0), stop=(k == IC - 1),
                          waits=w, inc=(1 if k == IC - 1 else 0))
            last_o = last
            tk = vec(lambda e, o=o, bk=bk: e.tensor_tensor(
                resid[:, o, :], h1[:, o, :], pool[bk][:, 0:T], OP.add),
                waits=[last, (misc["resid_free"] if o == 0 else None)])
            bank_tok[bk] = [tk]
            t_r.append(tk)
        misc["h1_free"] = t_r[-1]
        misc["y_free"] = [last_o]

        wst = {}

        def p_attn_stats():
            t_dresid = None
            if debug and idx == 0:
                t_dresid = dma("gpsimd", dbg["d_resid_0"],
                               resid.rearrange("p c t -> p (c t)"),
                               waits=[t_r[-1]])
            parA, t_mbf2, t_sbc2 = emit_ln_stats(resid, [t_r[-1]],
                                                 chunk_toks=t_r)
            wst["parA"], wst["t_mbf2"], wst["t_sbc2"] = parA, t_mbf2, t_sbc2
            wst["t_dresid"] = t_dresid

        def p_front():
            nb_ = nidx // NT
            if nb_ != b:
                misc["carry0"] = vec(lambda e: e.memset(carry, 0.0),
                                     waits=[t_y[IC - 1]])
                misc["xin0"] = vec(lambda e: e.memset(x_inner[:, :, 0:4], 0.0),
                                   waits=[t_ccar_cur])
            wst["nfront"] = emit_front(nidx)
            wst["ns1p"], wst["ns1out"] = emit_s1_pieces(wst["nfront"])

        def p_attn_apply():
            parA, t_mbf2, t_sbc2 = wst["parA"], wst["t_mbf2"], wst["t_sbc2"]
            t_dresid = wst["t_dresid"]
            mb3 = m_bf[parA].unsqueeze(1).broadcast_to([P, DC, T])
            t_p1b = vec(lambda e, mb3=mb3: e.tensor_tensor(
                sc_a, resid, mb3, OP.subtract),
                waits=[t_mbf2, t_h1a, misc["sc_a_free"],
                       (t_dresid if debug and idx == 0 else None)])
            mbf_free[parA] = t_p1b
            t_x1 = []
            t_sdump = None
            if debug and idx == 0:
                t_cps = vec(lambda e, parA=parA: e.tensor_copy(
                    out_sb[:, 0, :], paux[:, parA * T:(parA + 1) * T]),
                    waits=[t_sbc2])
                t_cpm = vec(lambda e, pA=pstat[parA][:, 0:T]: e.tensor_copy(
                    out_sb[:, 1, :], pA), waits=[t_mbf2, t_cps])
                dma("gpsimd", dbg["d_sd"], out_sb[:, 0, :], waits=[t_cpm])
                t_sdump = dma("gpsimd", dbg["d_md"], out_sb[:, 1, :],
                              waits=[t_cpm])
            t_dsca1 = None
            if debug and idx == 0:
                t_dsca1 = dma("gpsimd", dbg["d_sca1"],
                              sc_a.rearrange("p c t -> p (c t)"),
                              waits=[t_p1b])
            tts = []
            for c in range(DC):
                tt = vec(lambda e, c=c, b=b, parA=parA: e.scalar_tensor_tensor(
                    sc_a[:, c, :], sc_a[:, c, :], vv[:, c, b, 1:2],
                    paux[:, parA * T:(parA + 1) * T],
                    OP.mult, OP.mult),
                    waits=[t_p1b, t_sbc2, (t_dsca1 if c == 0 else None),
                           (t_sdump if c == 0 and t_sdump is not None
                            else None)])
                tts.append(tt)
            t_dsca2 = None
            if debug and idx == 0:
                t_dsca2 = dma("gpsimd", dbg["d_sca2"],
                              sc_a.rearrange("p c t -> p (c t)"),
                              waits=[tts[-1]])
            for c in range(DC):
                tk = vec(lambda e, c=c, xt=xt, b=b: e.scalar_tensor_tensor(
                    resid[:, c, :], sc_a[:, c, :], vv[:, c, b, 2:3],
                    xt[:, c, :], OP.add, OP.add),
                    waits=[tts[c], (t_dsca2 if c == 0 else None)])
                t_x1.append(tk)
            pst_free[parA] = pst_free[parA] + [t_p1b]
            sbc_free[parA] = [t_x1[-1]]
            misc["sc_a_free"] = t_x1[-1]
            misc["x_read"][front["par_t"]] = t_x1[-1]
            wst["t_x1"] = t_x1

        def p_ln2():
            t_x1 = wst["t_x1"]
            parC, t_mbf3, t_sbc3 = emit_ln_stats(resid, [t_x1[-1]],
                                                 chunk_toks=t_x1)
            wst["t_h2"] = emit_ln_apply(parC, t_mbf3, t_sbc3, resid, t_x1[-1],
                                       lambda c, b=b: vv[:, c, b, 6:7],
                                       lambda c, b=b: modv(3, c, b),
                                       h2, misc["h2_free"])

        def p_rtA():
            t_h2 = wst["t_h2"]
            last = None
            for k in range(DC):
                w = [t_h2[k]]
                if k == 0:
                    w += bank_tok[2] + [t_w["router"]]
                last = mm(pool[2][0:8, 0:T], router_w[:, k, :], h2[:, k, :],
                          start=(k == 0), stop=(k == DC - 1), waits=w,
                          inc=(1 if k == DC - 1 else 0))
            t_lg = vec(lambda e: e.tensor_scalar(lg_fm, pool[2][0:8, 0:T],
                                                 router_b, None, OP.add),
                       waits=[last, (t_sync if idx == 0 else None)])
            bank_tok[2] = [t_lg]
            wst["t_lg"] = t_lg

        def p_rtB():
            t_lg = wst["t_lg"]
            t_tp = []
            for tc in range(2):
                tk = tpose(prt[:, (T + tc * 8):(T + tc * 8 + 8)],
                           lg_fm[:, tc * P:(tc + 1) * P], ident[0:8, 0:8],
                           waits=[t_lg] + (prt_free if tc == 0 else []) +
                                 [(t_sync if idx == 0 and tc == 0 else None)])
                t_tp.append(tk)
            t_rt = act(lambda e: e.copy(
                rt, prt[:, T:T + 16].rearrange("p (a e) -> p a e", a=2)),
                waits=[t_tp[-1]])
            t_wtm_pre = []
            for tc in range(2):
                tk = vec(lambda e, tc=tc: e.reduce_max(
                    rm[:, tc, 0:1], rt[:, tc, :], axis=AX.X), waits=[t_rt])
                tk = vec(lambda e, tc=tc: e.tensor_scalar(
                    rmask[:, tc, :], rt[:, tc, :], rm[:, tc, 0:1], None,
                    OP.is_lt), waits=[tk])
                tk = vec(lambda e, tc=tc: e.tensor_scalar(
                    rmask[:, tc, :], rmask[:, tc, :], 1e9, -1e9, OP.mult,
                    OP.add), waits=[tk])
                tk = vec(lambda e, tc=tc: e.tensor_tensor(
                    rmask[:, tc, :], rmask[:, tc, :], rt[:, tc, :], OP.add),
                    waits=[tk])
                tk = vec(lambda e, tc=tc: e.reduce_max(
                    rm[:, tc, 1:2], rmask[:, tc, :], axis=AX.X), waits=[tk])
                tk = vec(lambda e, tc=tc: e.tensor_tensor(
                    rm[:, tc, 2:3], rm[:, tc, 1:2], rm[:, tc, 0:1],
                    OP.subtract), waits=[tk])
                t_wtm_pre.append(tk)
            wst["t_rt"], wst["t_wtm_pre"] = t_rt, t_wtm_pre

        def p_rtC():
            t_rt, t_wtm_pre = wst["t_rt"], wst["t_wtm_pre"]
            t_w1s = act(lambda e: e.activation(rm[:, :, 5:6], rm[:, :, 2:3],
                                               ACTF.Sigmoid, scale=-1.0),
                        waits=t_wtm_pre)
            t_w2s = act(lambda e: e.activation(rm[:, :, 6:7], rm[:, :, 2:3],
                                               ACTF.Sigmoid))
            t_wtm = []
            for tc in range(2):
                tk = vec(lambda e, tc=tc: e.tensor_scalar(
                    w_tm[:, tc, :], rt[:, tc, :], rm[:, tc, 0:1], None,
                    OP.is_equal), waits=[t_w1s, t_w2s])
                tk = vec(lambda e, tc=tc: e.tensor_scalar(
                    rmask[:, tc, :], rt[:, tc, :], rm[:, tc, 1:2], None,
                    OP.is_equal), waits=[tk])
                tk = vec(lambda e, tc=tc: e.tensor_scalar(
                    rmask[:, tc, :], rmask[:, tc, :], rm[:, tc, 6:7], None,
                    OP.mult), waits=[tk])
                tk = vec(lambda e, tc=tc: e.scalar_tensor_tensor(
                    w_tm[:, tc, :], w_tm[:, tc, :], rm[:, tc, 5:6],
                    rmask[:, tc, :], OP.mult, OP.add), waits=[tk])
                t_wtm.append(tk)
            wst["t_wtm"] = t_wtm

        def p_rtD():
            t_rt, t_wtm = wst["t_rt"], wst["t_wtm"]
            t_wfm = []
            t_wfm_ev = None
            for tc in range(2):
                tk = tpose(prt[0:8, (T + 32):(T + 32 + P)],
                           w_tm[:, tc, :], ident,
                           waits=[t_wtm[tc], t_wfm_ev])
                t_wfm_ev = vec(lambda e, tc=tc: e.tensor_scalar(
                    w_fm[:, tc * P:(tc + 1) * P],
                    prt[0:8, (T + 32):(T + 32 + P)],
                    2.0, None, OP.mult), waits=[tk])
                t_wfm.append(t_wfm_ev)
            wst["t_wfm"] = t_wfm
            wst["prt_free_mid"] = [t_rt, t_wfm[-1]]

        def p_wexpax():
            t_h2 = wst["t_h2"]
            t_wfm = wst["t_wfm"]
            t_wexp = mm(prt[:, 0:T], eexp, w_fm, start=True, stop=True,
                        waits=[t_wfm[-1]] + wst["prt_free_mid"] +
                              [(t_sync if idx == 0 else None)], inc=1)
            last = None
            for k in range(DC):
                w = [t_h2[k]]
                if k == 0:
                    w += bank_tok[3] + [t_w["lora_a"]]
                last = mm(pool[3][:, 0:T], lora_a[:, k, :], h2[:, k, :],
                          start=(k == 0), stop=(k == DC - 1), waits=w,
                          inc=(1 if k == DC - 1 else 0))
            t_axe = act(lambda e: e.copy(ax_bf, pool[3][:, 0:T]),
                        waits=[last])
            bank_tok[3] = [t_axe]
            t_axs = vec(lambda e: e.tensor_tensor(axs, ax_bf, prt[:, 0:T],
                                                  OP.mult),
                        waits=[t_axe, t_wexp])
            wst["t_axs"] = t_axs
            wst["prt_free_new"] = wst["prt_free_mid"] + [t_axs]

        t_out = []

        def p_ffn(o):
            t_h2, t_axs, t_x1 = wst["t_h2"], wst["t_axs"], wst["t_x1"]
            bk = 2 + (o % 2)
            last = None
            for k in range(DC):
                w = [t_h2[k]]
                if k == 0:
                    w += bank_tok[bk] + [t_w["base_w"], t_w["lora_b"],
                                         t_w["base_b"]]
                last = mm(pool[bk][:, 0:T], base_w[:, k, o * P:(o + 1) * P],
                          h2[:, k, :], start=(k == 0), stop=False, waits=w)
            last = mm(pool[bk][:, 0:T], base_b_row[0:1, o * P:(o + 1) * P],
                      h2ones, start=False, stop=False,
                      waits=[(t_h2o if idx == 0 and o == 0 else None)])
            last = mm(pool[bk][:, 0:T], lora_b[:, o * P:(o + 1) * P], axs,
                      start=False, stop=True, waits=[t_axs], inc=1)
            tk = vec(lambda e, o=o, bk=bk, b=b: e.scalar_tensor_tensor(
                out_sb[:, o, :], pool[bk][:, 0:T], vv[:, o, b, 3:4],
                resid[:, o, :], OP.mult, OP.add),
                waits=[last, t_x1[o],
                       (misc["store"][0] if o == 0 else None),
                       (misc["store"][1] if o == 4 else None),
                       (t_bT if idx == 0 and o == 0 else None)])
            bank_tok[bk] = [tk]
            t_out.append(tk)
            if o == 3:
                wst["t_st0"] = dma("scalar", out_tile_ap(b, j)[:, 0:4, :],
                                   out_sb[:, 0:4, :], waits=[tk])
            if o == DC - 1:
                misc["resid_free"] = tk
                misc["h2_free"] = last

        def p_store():
            t_store = dma("scalar", out_tile_ap(b, j)[:, 4:8, :],
                          out_sb[:, 4:8, :], waits=[t_out[-1]])
            misc["store"] = [wst["t_st0"], t_store]

        if nidx < NB * NT:
            p_attn_stats(); p_front()
            ns1 = wst["ns1p"]
            p_attn_apply()
            p_ln2()
            ns1[0](); ns1[1]()
            p_rtA()
            ns1[2](); ns1[3](); ns1[4]()
            p_rtB(); p_rtC(); p_rtD()
            ns1[5](); ns1[6]()
            p_wexpax()
            ns1[7](); ns1[8](); ns1[9]()
            for o in range(DC - 1):
                p_ffn(o)
                ns1[10 + o]()
            p_ffn(DC - 1)
            p_store()
            prt_free = wst["prt_free_new"]
            front = wst["nfront"]
            s1out = wst["ns1out"]
        else:
            p_attn_stats(); p_attn_apply(); p_ln2()
            p_rtA(); p_rtB(); p_rtC(); p_rtD(); p_wexpax()
            for o in range(DC):
                p_ffn(o)
            p_store()
            prt_free = wst["prt_free_new"]

        if debug and idx == 0:
            dma("scalar", dbg["d_mod"], mod_acc, waits=[t_mod])
            dma("scalar", dbg["d_vv"],
                vv.rearrange("p c b v -> p (c b v)"),
                waits=[misc["bprep"][0]])
            dma("gpsimd", dbg["d_x1_0"], resid.rearrange("p c t -> p (c t)"),
                waits=[wst["t_x1"][-1]])
            dma("gpsimd", dbg["d_y_0"], y_bf.rearrange("p c t -> p (c t)"),
                waits=[t_y[-1]])
            dma("gpsimd", dbg["d_h2_0"], h2.rearrange("p c t -> p (c t)"),
                waits=[wst["t_h2"][-1], misc["h2_free"]])
            dma("gpsimd", dbg["d_xc_0"], x_conv.rearrange("p c t -> p (c t)"),
                waits=[t_xconv[-1], t_y[-1]])
            dma("scalar", dbg["d_wtm_0"],
                w_tm.rearrange("p a e -> p (a e)"), waits=[wst["t_wtm"][-1]])

    S.emit("scalar", lambda e: e.nop(),
           waits=[misc["store"][0], misc["store"][1]], inc=0)

    with nc.Block() as block:
        def runner(ename):
            def body(eng):
                waited = {e: 0 for e in ENGS}
                for ws, fn, inc in S.ops[ename]:
                    for (we, wv) in ws:
                        if wv <= waited[we]:
                            continue
                        eng.wait_ge(sems[we], wv)
                        waited[we] = wv
                    inst = fn(eng)
                    if inc:
                        inst.then_inc(sems[ename], inc)
            return body

        block.sync(runner("sync"))
        block.gpsimd(runner("gpsimd"))
        block.vector(runner("vector"))
        block.scalar(runner("scalar"))
        block.tensor(runner("tensor"))

    return nc, st, S


def _chunked(a, c):
    d, e = a.shape
    return np.ascontiguousarray(a.reshape(c, P, e).transpose(1, 0, 2)
                                .reshape(P, -1))


def _vec_chunk(v):
    return np.ascontiguousarray(v.reshape(-1, P).T)


def make_in_maps(L0_attn_x, L0_attn_W_in, L0_attn_conv_w, L0_attn_conv_b,
                 L0_attn_W_x, L0_attn_B_log, L0_attn_W_out, L0_attn_ln_g,
                 L0_attn_ln_b, L0_ffn_base_w, L0_ffn_base_b, L0_ffn_lora_A,
                 L0_ffn_lora_B, L0_ffn_router_w, L0_ffn_router_b, L0_c,
                 L0_wmod):
    f = lambda a: np.asarray(a, dtype=np.float32)
    x = np.ascontiguousarray(f(L0_attn_x).transpose(0, 2, 1))
    w_in = _chunked(f(L0_attn_W_in).T, DC)
    w_out = _chunked(f(L0_attn_W_out).T, IC)
    base_w = _chunked(f(L0_ffn_base_w).T, DC)
    lora_a = _chunked(f(L0_ffn_lora_A).transpose(1, 0, 2).reshape(D, P), DC)
    lora_b = np.ascontiguousarray(f(L0_ffn_lora_B).reshape(P, D))
    w_x_c = _chunked(f(L0_attn_W_x)[16:32, :].T, IC)
    router_wt = _chunked(f(L0_ffn_router_w).T, DC)
    router_b = np.ascontiguousarray(f(L0_ffn_router_b).reshape(8, 1))
    b_logt = np.ascontiguousarray(f(L0_attn_B_log).T)
    conv_w = _chunked(f(L0_attn_conv_w).reshape(DI, 4), IC)
    conv_b = _vec_chunk(f(L0_attn_conv_b))
    ln_g = _vec_chunk(f(L0_attn_ln_g))
    ln_b = _vec_chunk(f(L0_attn_ln_b))
    base_b_row = np.ascontiguousarray(f(L0_ffn_base_b).reshape(1, D))
    c_all = _chunked(f(L0_c).T, DC).reshape(P, DC, 16)
    wmod = np.ascontiguousarray(f(L0_wmod))
    ident = np.eye(P, dtype=np.float32)
    eexp = np.zeros((8, P), dtype=np.float32)
    for e in range(8):
        eexp[e, e * 16:(e + 1) * 16] = 1.0
    eexp = eexp.astype(ml_dtypes.bfloat16)

    in_maps = []
    for core in range(8):
        bsl = slice(2 * core, 2 * core + 2)
        in_maps.append({
            "x": np.ascontiguousarray(x[bsl]),
            "w_in": w_in, "w_out": w_out, "base_w": base_w,
            "lora_a": lora_a, "lora_b": lora_b, "w_x_c": w_x_c,
            "router_wt": router_wt, "router_b": router_b,
            "b_logt": b_logt, "conv_w": conv_w, "conv_b": conv_b,
            "ln_g": ln_g, "ln_b": ln_b, "base_b_row": base_b_row,
            "c_t": np.ascontiguousarray(c_all[:, :, bsl].reshape(P, -1)),
            "wmod": wmod, "ident": ident, "eexp": eexp,
        })
    return in_maps


def kernel(**inputs):
    in_maps = make_in_maps(**inputs)
    nc, stack, _ = build_program()
    try:
        res = run_bass_kernel_spmd(nc, in_maps, core_ids=list(range(8)))
    finally:
        stack.close()
    outs = []
    for core in range(8):
        o = np.asarray(res.results[core]["out"])
        outs.append(o.transpose(0, 2, 1))
    return np.ascontiguousarray(np.concatenate(outs, axis=0),
                                dtype=np.float32)


# version 34
# speedup vs baseline: 1.0451x; 1.0113x over previous
import numpy as np
import ml_dtypes
from contextlib import ExitStack

import concourse.bass as bass
import concourse.mybir as mybir
from concourse.bass_utils import run_bass_kernel_spmd

F32 = mybir.dt.float32
BF16 = mybir.dt.bfloat16
AX = mybir.AxisListType
OP = mybir.AluOpType
ACTF = mybir.ActivationFunctionType

P = 128
T = 256
D = 1024
DI = 2048
DC = D // P
IC = DI // P
NT = 8
NB = 2
EPS = 1e-5
DECAY = 0.95

ENGS = ("sync", "gpsimd", "vector", "scalar", "tensor")


class Sched:
    def __init__(self):
        self.ops = {e: [] for e in ENGS}
        self.count = {e: 0 for e in ENGS}

    def emit(self, eng, fn, waits=(), inc=0):
        ws = tuple(w for w in waits if w is not None)
        self.ops[eng].append((ws, fn, inc))
        if inc:
            self.count[eng] += inc
            return (eng, self.count[eng])
        return None


def build_program(debug=False):
    nc = bass.Bass("TRN2", target_bir_lowering=False, debug=False)
    st = ExitStack()
    dr = {}

    def din(name, shape, dt=F32):
        dr[name] = nc.dram_tensor(name, list(shape), dt, kind="ExternalInput").ap()
        return dr[name]

    din("x", [NB, D, NT * T])
    din("w_in", [P, DC * 2 * DI])
    din("w_out", [P, IC * D])
    din("base_w", [P, DC * D])
    din("lora_a", [P, DC * P])
    din("lora_b", [P, D])
    din("w_x_c", [P, IC * 16])
    din("router_wt", [P, DC * 8])
    din("router_b", [8, 1])
    din("b_logt", [16, DI])
    din("conv_w", [P, IC * 4])
    din("conv_b", [P, IC])
    din("ln_g", [P, DC])
    din("ln_b", [P, DC])
    din("base_b_row", [1, D])
    din("c_t", [P, DC * NB])
    din("wmod", [D, 6 * D])
    din("ident", [P, P])
    din("eexp", [8, P], BF16)

    out_d = nc.dram_tensor("out", [NB, D, NT * T], F32, kind="ExternalOutput").ap()

    dbg = {}
    if debug:
        for nm, shp in [("d_mod", [P, 96]), ("d_vv", [P, DC * 2 * 8]),
                        ("d_h1_0", [P, DC * T]), ("d_h1_8", [P, DC * T]),
                        ("d_x1_0", [P, DC * T]), ("d_h2_0", [P, DC * T]),
                        ("d_xc_0", [P, IC * T]), ("d_wtm_0", [P, 2 * 8]),
                        ("d_resid_0", [P, DC * T]), ("d_y_0", [P, IC * T]),
                        ("d_sca1", [P, DC * T]), ("d_sca2", [P, DC * T]),
                        ("d_sd", [P, T]), ("d_md", [P, T])]:
            dbg[nm] = nc.dram_tensor(nm, shp, F32, kind="ExternalOutput").ap()

    def sb(name, shape, dt):
        return st.enter_context(nc.sbuf_tensor(name, list(shape), dt))[:]

    w_in = sb("w_in_sb", [P, DC, 2 * DI], BF16)
    w_out = sb("w_out_sb", [P, IC, D], BF16)
    base_w = sb("base_w_sb", [P, DC, D], BF16)
    lora_a = sb("lora_a_sb", [P, DC, P], BF16)
    lora_b = sb("lora_b_sb", [P, D], BF16)
    w_x_c = sb("w_x_c_sb", [P, IC, 16], BF16)
    router_w = sb("router_w_sb", [P, DC, 8], BF16)
    bT = sb("bT_sb", [16, DI], BF16)
    base_b_row = sb("base_b_row_sb", [1, D], BF16)

    conv_w = sb("conv_w_sb", [P, IC, 4], F32)
    conv_b = sb("conv_b_sb", [P, IC], F32)
    ln_g = sb("ln_g_sb", [P, DC], F32)
    ln_b = sb("ln_b_sb", [P, DC], F32)
    router_b = sb("router_b_sb", [8, 1], F32)
    ident = sb("ident_sb", [P, P], F32)
    eexp = sb("eexp_sb", [8, P], BF16)
    onesd = sb("onesd_sb", [P, P], BF16)
    ones1 = sb("ones1_sb", [1, P], BF16)
    h2ones = sb("h2ones_sb", [1, T], BF16)
    dec = sb("dec_sb", [P, T], F32)
    eps_sb = sb("eps_sb", [P, 1], F32)

    c_t = sb("c_t_sb", [P, DC, NB], F32)
    siluc = sb("siluc_sb", [P, DC, NB], BF16)
    mod_acc = sb("mod_acc_sb", [P, 96], F32)
    vv = sb("vv_sb", [P, DC, 2, 8], F32)

    big0 = sb("big0_sb", [P, 3 * D], F32)
    big0_bf = big0.bitcast(BF16)
    wm_half = [big0_bf[:, 0:3072], big0_bf[:, 3072:6144]]
    sq = big0_bf[:, 0:2048].rearrange("p (c t) -> p c t", c=DC)
    y_bf = big0_bf[:, 2048:6144].rearrange("p (c t) -> p c t", c=IC)

    x_sb = sb("x_sb", [P, 2, DC, T], BF16)
    sc_a = sb("sc_a_sb", [P, DC, T], BF16)
    resid = sb("resid_sb", [P, DC, T], BF16)
    h1 = sb("h1_sb", [P, DC, T], BF16)
    h2 = sb("h2_sb", [P, DC, T], BF16)
    x_inner = sb("x_inner_sb", [P, IC, T + 8], BF16)
    cacc = sb("cacc_sb", [P, 4, T], BF16)
    cb_sb = sb("cb_sb", [P, 3, T], BF16)
    m_bf = [sb("m_bfA", [P, T], BF16), sb("m_bfB", [P, T], BF16)]
    x_conv = sb("x_conv_sb", [P, IC, T], BF16)
    zsilu = sb("zsilu_sb", [P, 3, T], BF16)
    hema = sb("hema_sb", [P, 3, T], BF16)
    y1 = sb("y1_sb", [P, 3, T], BF16)
    s_c = sb("s_c_sb", [16, T], BF16)
    carry = sb("carry_sb", [P, IC], F32)
    out_sb = sb("out_sb", [P, DC, T], F32)

    st_mm = [sb("st_mmA", [1, T], F32), sb("st_mmB", [1, T], F32)]
    st_vpe = [sb("st_vpeA", [1, T], F32), sb("st_vpeB", [1, T], F32)]
    st_s = [sb("st_sA", [1, T], BF16), sb("st_sB", [1, T], BF16)]

    lg_fm = sb("lg_fm_sb", [8, T], F32)
    rt = sb("rt_sb", [P, 2, 8], F32)
    rm = sb("rm_sb", [P, 2, 8], F32)
    rmask = sb("rmask_sb", [P, 2, 8], F32)
    w_tm = sb("w_tm_sb", [P, 2, 8], F32)
    w_fm = sb("w_fm_sb", [8, T], BF16)
    ax_bf = sb("ax_bf_sb", [P, T], BF16)
    axs = sb("axs_sb", [P, T], BF16)

    blog_stage = out_sb.rearrange("p c t -> p (c t)")[0:16, 0:DI]

    def psum(name):
        return st.enter_context(nc.psum_tensor(name, [P, 512], F32))[:]

    pool = [psum("pool0"), psum("pool1"), psum("pool2"), psum("pool3")]
    pstat = [psum("pstatA"), psum("pstatB")]
    paux = psum("paux")
    prt = psum("prt")

    sems = {e: st.enter_context(nc.semaphore("s_" + e)) for e in ENGS}
    S = Sched()

    def dma(queue, outap, inap, waits=(), inc=16):
        return S.emit(queue, lambda e, o=outap, i=inap: e.dma_start(out=o, in_=i),
                      waits=waits, inc=inc)

    def mm(outap, lhsT, rhs, start, stop, waits=(), inc=0):
        return S.emit("tensor",
                      lambda e, o=outap, l=lhsT, r=rhs, a=start, b=stop:
                      e.matmul(o, l, r, start=a, stop=b),
                      waits=waits, inc=inc)

    def tpose(outap, inap, idap, waits=(), inc=1):
        return S.emit("tensor",
                      lambda e, o=outap, i=inap, d=idap: e.transpose(o, i, d),
                      waits=waits, inc=inc)

    def act(fn, waits=(), inc=1):
        return S.emit("scalar", fn, waits=waits, inc=inc)

    def vec(fn, waits=(), inc=1):
        return S.emit("vector", fn, waits=waits, inc=inc)

    for nm, dst in [("conv_b", conv_b), ("ln_g", ln_g), ("ln_b", ln_b),
                    ("router_b", router_b), ("ident", ident), ("eexp", eexp)]:
        dma("sync", dst, dr[nm])
    dma("sync", conv_w, dr["conv_w"].rearrange("p (c k) -> p c k", c=IC))
    t_ct = dma("sync", c_t, dr["c_t"].rearrange("p (c b) -> p c b", c=DC))
    t_blog = dma("sync", blog_stage, dr["b_logt"])
    t_sync = t_blog

    t_onesd = vec(lambda e: e.memset(onesd, 1.0 / D))
    t_ones1 = vec(lambda e: e.memset(ones1, 1.0))
    t_h2o = vec(lambda e: e.memset(h2ones, 1.0))
    t_dec = vec(lambda e: e.memset(dec, DECAY))
    t_eps = vec(lambda e: e.memset(eps_sb, EPS))

    def ka(wtok):
        S.emit("tensor",
               lambda e: e.matmul(pool[3][:, 256:384], onesd, onesd,
                                  start=True, stop=True),
               waits=[wtok], inc=0)

    last = None
    for i in range(72):
        last = mm(prt[:, 0:P], onesd, onesd, start=True, stop=True,
                  waits=([t_onesd] if i == 0 else ()),
                  inc=(1 if i == 71 else 0))
    t_warm = last

    t_siluc = act(lambda e: e.activation(siluc, c_t, ACTF.Silu), waits=[t_ct])
    t_bT = act(lambda e: e.activation(bT, blog_stage, ACTF.Exp), waits=[t_blog])

    x_view = dr["x"]

    def x_tile_ap(b, j):
        return x_view[b, :, j * T:(j + 1) * T].rearrange("(c p) t -> p c t", p=P)

    def out_tile_ap(b, j):
        return out_d[b, :, j * T:(j + 1) * T].rearrange("(c p) t -> p c t", p=P)

    t_x = [None, None]
    t_x[0] = dma("gpsimd", x_sb[:, 0], x_tile_ap(0, 0))

    t_mm_half = [None, None]
    t_macc = None
    for k in range(DC):
        for h in range(2):
            t_slab = dma("gpsimd", wm_half[h],
                         dr["wmod"][k * P:(k + 1) * P, h * 3072:(h + 1) * 3072],
                         waits=[t_mm_half[h], (t_macc if h == 0 else None)])
            last = None
            for mci in range(24):
                mc = h * 24 + mci
                w = [t_slab, t_siluc, t_warm] if mci == 0 else ()
                last = mm(prt[:, mc * NB:(mc + 1) * NB],
                          wm_half[h][:, mci * P:(mci + 1) * P], siluc[:, k, :],
                          start=True, stop=True, waits=w,
                          inc=(1 if mci == 23 else 0))
            t_mm_half[h] = last
        if k == 0:
            t_macc = vec(lambda e: e.tensor_copy(mod_acc, prt[:, 0:96]),
                         waits=[t_mm_half[0], t_mm_half[1]])
        else:
            t_macc = vec(lambda e: e.tensor_tensor(mod_acc, mod_acc,
                                                   prt[:, 0:96], OP.add),
                         waits=[t_mm_half[0], t_mm_half[1]])
    t_mod = t_macc
    t_mod_mm = t_mm_half[1]

    t_w = {}
    t_w["w_in"] = dma("gpsimd", w_in, dr["w_in"].rearrange("p (c e) -> p c e", c=DC))
    t_w["w_out"] = dma("gpsimd", w_out, dr["w_out"].rearrange("p (c e) -> p c e", c=IC))
    t_w["base_w"] = dma("gpsimd", base_w, dr["base_w"].rearrange("p (c e) -> p c e", c=DC))
    t_w["lora_a"] = dma("gpsimd", lora_a, dr["lora_a"].rearrange("p (c e) -> p c e", c=DC))
    t_w["lora_b"] = dma("gpsimd", lora_b, dr["lora_b"])
    t_w["w_x_c"] = dma("gpsimd", w_x_c, dr["w_x_c"].rearrange("p (c e) -> p c e", c=IC))
    t_w["router"] = dma("gpsimd", router_w, dr["router_wt"].rearrange("p (c e) -> p c e", c=DC))
    t_w["base_b"] = dma("gpsimd", base_b_row, dr["base_b_row"])

    def modv(vi, c, b):
        mc = vi * DC + c
        return mod_acc[:, (mc * NB + b):(mc * NB + b + 1)]

    def emit_batch_prep(b, wait0):
        tk = wait0
        for c in range(DC):
            gav, gbv = modv(2, c, b), modv(5, c, b)
            vb = vv[:, c, b]
            if c == 0:
                tk = vec(lambda e: e.engine_nop(), waits=[tk, t_sync])
            tk = vec(lambda e, i=gav, o=vb[:, 0:1]: e.tensor_scalar(
                o, i, -1.0, None, OP.mult), waits=[tk])
            tk = vec(lambda e, i=gav, o=vb[:, 0:1]: e.tensor_tensor(
                o, o, i, OP.max), waits=[tk])
            tk = vec(lambda e, o=vb[:, 0:1]: e.tensor_scalar(
                o, o, 0.25, None, OP.max), waits=[tk])
            tk = vec(lambda e, i=gav, o=vb[:, 7:8]: e.tensor_scalar(
                o, i, 0.0, 2.0, OP.is_ge, OP.mult), waits=[tk])
            tk = vec(lambda e, o=vb[:, 7:8]: e.tensor_scalar(
                o, o, 1.0, None, OP.subtract), waits=[tk])
            tk = vec(lambda e, o=vb[:, 0:1], s=vb[:, 7:8]: e.tensor_tensor(
                o, o, s, OP.mult), waits=[tk])
            tk = vec(lambda e, o=vb[:, 1:2], a=vb[:, 0:1], g=ln_g[:, c:c + 1]:
                     e.tensor_tensor(o, a, g, OP.mult), waits=[tk])
            tk = vec(lambda e, o=vb[:, 2:3], a=vb[:, 0:1], g=ln_b[:, c:c + 1]:
                     e.tensor_tensor(o, a, g, OP.mult), waits=[tk])
            tk = vec(lambda e, i=gbv, o=vb[:, 3:4]: e.tensor_scalar(
                o, i, -1.0, None, OP.mult), waits=[tk])
            tk = vec(lambda e, i=gbv, o=vb[:, 3:4]: e.tensor_tensor(
                o, o, i, OP.max), waits=[tk])
            tk = vec(lambda e, o=vb[:, 3:4]: e.tensor_scalar(
                o, o, 0.25, None, OP.max), waits=[tk])
            tk = vec(lambda e, i=gbv, o=vb[:, 7:8]: e.tensor_scalar(
                o, i, 0.0, 2.0, OP.is_ge, OP.mult), waits=[tk])
            tk = vec(lambda e, o=vb[:, 7:8]: e.tensor_scalar(
                o, o, 1.0, None, OP.subtract), waits=[tk])
            tk = vec(lambda e, o=vb[:, 3:4], s=vb[:, 7:8]: e.tensor_tensor(
                o, o, s, OP.mult), waits=[tk])
            tk = vec(lambda e, i=modv(1, c, b), o=vb[:, 5:6]: e.tensor_scalar(
                o, i, 1.0, None, OP.add), waits=[tk])
            tk = vec(lambda e, i=modv(4, c, b), o=vb[:, 6:7]: e.tensor_scalar(
                o, i, 1.0, None, OP.add), waits=[tk])
        return tk

    bank_tok = [[], [], [], []]
    pst_free = [[], []]
    stm_free = [None, None]
    mbf_free = [None, None]
    sbc_free = [[t_mod_mm], [t_mod_mm]]
    prt_free = [t_mod_mm, t_mod]
    misc = dict(sq_free=[t_mod_mm, t_mod], y_free=[t_mod_mm, t_mod],
                sc_a_free=None, h1_free=None, h2_free=None, resid_free=None,
                xin_free=None, xc_free=None, store=None,
                zs_free=[None, None, None], hema_free=[None, None, None],
                y1_free=[None, None, None], cacc_free=[None, None, None, None],
                cp_free=[None, None],
                cbe_free=[None, None],
                x_read=[None, None], bprep=[None, None],
                carry0=None, xin0=None)

    ln_seq = [0]

    def emit_ln_stats(src, src_toks, pe_extra=(), chunk_toks=None):
        par = ln_seq[0] % 2
        ln_seq[0] += 1
        ctok = chunk_toks if chunk_toks is not None else [None] * DC
        t_sq_c = [None] * DC
        for c in range(0, DC, 2):
            w = [ctok[c + 1] if chunk_toks is not None else None]
            if c == 0:
                w += list(src_toks) if chunk_toks is None else []
                w += misc["sq_free"]
            tk = act(lambda e, c=c: e.activation(
                sq[:, c:c + 2, :], src[:, c:c + 2, :], ACTF.Square), waits=w)
            t_sq_c[c] = t_sq_c[c + 1] = tk
        last = None
        for c in range(DC):
            w = [ctok[c] if chunk_toks is not None else None]
            if c == 0:
                w += list(pe_extra) + pst_free[par]
                w += list(src_toks) if chunk_toks is None else []
            last = mm(pstat[par][:, 0:T], onesd, src[:, c, :],
                      start=(c == 0), stop=(c == DC - 1), waits=w)
        for c in range(DC):
            last = mm(pstat[par][:, T:2 * T], onesd, sq[:, c, :],
                      start=(c == 0), stop=(c == DC - 1), waits=[t_sq_c[c]],
                      inc=(1 if c == DC - 1 else 0))
        t_sums = last
        misc["sq_free"] = [t_sums]
        del t_sq_c
        t_mbf = act(lambda e, par=par: e.copy(m_bf[par], pstat[par][:, 0:T]),
                    waits=[t_sums, mbf_free[par]])
        t_mmr = act(lambda e, par=par: e.activation(
            st_mm[par], pstat[par][0:1, 0:T], ACTF.Square),
            waits=[stm_free[par]])
        t_vpe = vec(lambda e, par=par: e.scalar_tensor_tensor(
            st_vpe[par], st_mm[par], -1.0, pstat[par][0:1, T:2 * T],
            OP.mult, OP.add), waits=[t_mmr])
        t_ln = act(lambda e, par=par: e.activation(
            st_vpe[par], st_vpe[par], ACTF.Ln, bias=eps_sb[0:1, 0:1]),
            waits=[t_vpe, t_eps])
        t_s = act(lambda e, par=par: e.activation(
            st_s[par], st_vpe[par], ACTF.Exp, scale=-0.5), waits=[t_ln])
        t_sbc = mm(paux[:, par * T:(par + 1) * T], ones1, st_s[par],
                   start=True, stop=True,
                   waits=[t_s, t_ones1] + sbc_free[par], inc=1)
        pst_free[par] = [t_vpe, t_mmr]
        stm_free[par] = t_vpe
        return par, t_mmr, t_sbc

    def emit_ln_apply(par, t_mbf, t_sbc, src, src_tok, gcol, acol, dst,
                      dst_free, extra0=None):
        mb3 = m_bf[par].unsqueeze(1).broadcast_to([P, DC, T])
        t_p1 = vec(lambda e, src=src, mb3=mb3: e.tensor_tensor(
            sc_a, src, mb3, OP.subtract),
            waits=[t_mbf, src_tok, misc["sc_a_free"], extra0])
        mbf_free[par] = t_p1
        toks = []
        for c in range(DC):
            tt = vec(lambda e, c=c, par=par: e.scalar_tensor_tensor(
                sc_a[:, c, :], sc_a[:, c, :], gcol(c),
                paux[:, par * T:(par + 1) * T],
                OP.mult, OP.mult), waits=[t_p1, t_sbc])
            tk = act(lambda e, c=c, dst=dst: e.activation(
                dst[:, c, :], sc_a[:, c, :], ACTF.Identity, bias=acol(c)),
                waits=[tt, (dst_free if c == 0 else None)])
            toks.append(tk)
        pst_free[par] = pst_free[par] + [t_p1]
        sbc_free[par] = [toks[-1]]
        misc["sc_a_free"] = toks[-1]
        return toks

    def emit_front(idx):
        b, j = divmod(idx, NT)
        par_t = idx % 2
        xt = x_sb[:, par_t]
        t_xl = t_x[par_t]
        pe_extra = [t_onesd] if idx == 0 else []
        par, t_mbf1, t_sbc1 = emit_ln_stats(xt, [t_xl], pe_extra)
        t_h1 = emit_ln_apply(par, t_mbf1, t_sbc1, xt, t_xl,
                             lambda c, b=b: vv[:, c, b, 5:6],
                             lambda c, b=b: modv(0, c, b),
                             h1, misc["h1_free"],
                             extra0=misc["bprep"][b])
        return dict(idx=idx, b=b, j=j, par_t=par_t, xt=xt, t_h1=t_h1)

    misc["bprep"][0] = emit_batch_prep(0, t_mod)
    misc["bprep"][1] = emit_batch_prep(1, misc["bprep"][0])
    misc["carry0"] = vec(lambda e: e.memset(carry, 0.0))
    misc["xin0"] = vec(lambda e: e.memset(x_inner[:, :, 0:4], 0.0))

    def emit_s1_pieces(fr):
        s1idx, jj = fr["idx"], fr["j"]
        t_h1a = fr["t_h1"][-1]
        out = dict(t_xconv=[None] * IC, t_mlast=None, t_ccar=None)

        def chunk(c):
            bk = c % 2
            last = None
            for k in range(DC):
                w = []
                if k == 0:
                    w = list(bank_tok[bk])
                    if c == 0:
                        w += [t_h1a, t_w["w_in"]]
                last = mm(pool[bk][:, 0:T], w_in[:, k, c * P:(c + 1) * P],
                          h1[:, k, :], start=(k == 0), stop=(k == DC - 1),
                          waits=w, inc=(1 if k == DC - 1 else 0))
            w_ev = [last]
            if c < 4:
                w_ev += [misc["xin0"] if jj == 0 else None, misc["xin_free"]]
            t_ev = act(lambda e, c=c, bk=bk: e.copy(
                x_inner[:, c, 4:4 + T], pool[bk][:, 0:T]), waits=w_ev)
            bank_tok[bk] = [t_ev]
            rp = c % 4
            tk = vec(lambda e, c=c, rp=rp: e.tensor_scalar(
                cacc[:, rp, :], x_inner[:, c, 4:4 + T], conv_w[:, c, 3:4],
                None, OP.mult),
                waits=[t_ev, misc["cacc_free"][rp],
                       (t_sync if s1idx == 0 and c == 0 else None)])
            tk = vec(lambda e, c=c, rp=rp: e.scalar_tensor_tensor(
                cacc[:, rp, :], x_inner[:, c, 3:3 + T], conv_w[:, c, 2:3],
                cacc[:, rp, :], OP.mult, OP.add), waits=[tk])
            tk = vec(lambda e, c=c, rp=rp: e.scalar_tensor_tensor(
                cacc[:, rp, :], x_inner[:, c, 2:2 + T], conv_w[:, c, 1:2],
                cacc[:, rp, :], OP.mult, OP.add), waits=[tk])
            tk = vec(lambda e, c=c, rp=rp: e.scalar_tensor_tensor(
                cacc[:, rp, :], x_inner[:, c, 1:1 + T], conv_w[:, c, 0:1],
                cacc[:, rp, :], OP.mult, OP.add), waits=[tk])
            out["t_mlast"] = tk
            t_xc = act(lambda e, c=c, rp=rp: e.activation(
                x_conv[:, c, :], cacc[:, rp, :], ACTF.Silu,
                bias=conv_b[:, c:c + 1]),
                waits=[tk, (misc["xc_free"] if c < 2 else None)])
            misc["cacc_free"][rp] = t_xc
            out["t_xconv"][c] = t_xc

        def carry_piece():
            t_ccar = vec(lambda e: e.tensor_copy(
                x_inner[:, :, 1:4], x_inner[:, :, T + 1:T + 4]),
                waits=[out["t_mlast"]])
            misc["xin_free"] = t_ccar
            out["t_ccar"] = t_ccar

        pieces = [(lambda c=c: chunk(c)) for c in range(IC)] + [carry_piece]
        return pieces, out

    front = emit_front(0)
    s1p, s1out = emit_s1_pieces(front)
    for p in s1p:
        p()

    for idx in range(NB * NT):
        b, j = front["b"], front["j"]
        xt = front["xt"]
        t_h1 = front["t_h1"]
        t_h1a = t_h1[-1]
        t_xconv = s1out["t_xconv"]
        t_ccar_cur = s1out["t_ccar"]

        nidx = idx + 1
        if nidx < NB * NT:
            nb_, nj_ = divmod(nidx, NT)
            t_x[nidx % 2] = dma("gpsimd", x_sb[:, nidx % 2],
                                x_tile_ap(nb_, nj_),
                                waits=[misc["x_read"][nidx % 2]])
        if debug and idx in (0, 8):
            dma("gpsimd", dbg["d_h1_0" if idx == 0 else "d_h1_8"],
                h1.rearrange("p c t -> p (c t)"), waits=[t_h1a])

        last = None
        for c in range(IC):
            w = [t_xconv[c]]
            if c == 0:
                w += prt_free + [t_w["w_x_c"]]
            last = mm(prt[0:16, 0:T], w_x_c[:, c, :], x_conv[:, c, :],
                      start=(c == 0), stop=(c == IC - 1), waits=w,
                      inc=(1 if c == IC - 1 else 0))
        t_sc = act(lambda e: e.copy(s_c, prt[0:16, 0:T]), waits=[last])
        prt_free = [t_sc]

        t_y = [None] * IC
        for c in range(IC):
            rp = c % 3
            cbp = c % 2
            zbk = 2 + (c % 2)
            cbs = pstat[cbp][:, 0:T]
            t_scan = vec(lambda e, c=c, rp=rp: e.tensor_tensor_scan(
                hema[:, rp, :], dec, x_conv[:, c, :], carry[:, c:c + 1],
                OP.mult, OP.add),
                waits=[t_xconv[c], misc["hema_free"][rp],
                       (misc["carry0"] if j == 0 else None), t_dec])
            vec(lambda e, c=c, rp=rp: e.tensor_copy(
                carry[:, c:c + 1], hema[:, rp, T - 1:T]), waits=[t_scan])
            t_cb = mm(cbs, bT[:, c * P:(c + 1) * P], s_c,
                      start=True, stop=True,
                      waits=[t_sc] + pst_free[cbp] +
                            [(t_bT if idx == 0 and c < 2 else None)],
                      inc=1)
            last = None
            for k in range(DC):
                w = list(bank_tok[zbk]) if k == 0 else ()
                last = mm(pool[zbk][:, 0:T],
                          w_in[:, k, (IC + c) * P:(IC + c + 1) * P],
                          h1[:, k, :], start=(k == 0), stop=(k == DC - 1),
                          waits=w, inc=(1 if k == DC - 1 else 0))
            t_z = act(lambda e, rp=rp, zb=zbk: e.activation(
                zsilu[:, rp, :], pool[zb][:, 0:T], ACTF.Silu),
                waits=[last, misc["zs_free"][rp]])
            bank_tok[zbk] = [t_z]
            t_cbe = act(lambda e, rp=rp, cb=cbs: e.copy(cb_sb[:, rp, :], cb),
                        waits=[t_cb, misc["y1_free"][rp]])
            pst_free[cbp] = [t_cbe]
            t_y1 = vec(lambda e, rp=rp: e.tensor_tensor(
                y1[:, rp, :], hema[:, rp, :], cb_sb[:, rp, :], OP.mult),
                waits=[t_scan, t_cbe, misc["y1_free"][rp]])
            misc["hema_free"][rp] = t_y1
            t_yc = vec(lambda e, c=c, rp=rp: e.tensor_tensor(
                y_bf[:, c, :], y1[:, rp, :], zsilu[:, rp, :], OP.mult),
                waits=[t_y1, t_z] + (misc["y_free"] if c < 2 else []))
            misc["y1_free"][rp] = t_yc
            misc["zs_free"][rp] = t_yc
            t_y[c] = t_yc
        misc["xc_free"] = t_y[IC - 1]

        t_r = []
        last_o = None
        for o in range(DC):
            bk = 2 + (o % 2)
            last = None
            for k in range(IC):
                w = [t_y[k]]
                if k == 0:
                    w += bank_tok[bk] + [t_w["w_out"]]
                last = mm(pool[bk][:, 0:T], w_out[:, k, o * P:(o + 1) * P],
                          y_bf[:, k, :], start=(k == 0), stop=(k == IC - 1),
                          waits=w, inc=(1 if k == IC - 1 else 0))
            last_o = last
            tk = vec(lambda e, o=o, bk=bk: e.tensor_tensor(
                resid[:, o, :], h1[:, o, :], pool[bk][:, 0:T], OP.add),
                waits=[last, (misc["resid_free"] if o == 0 else None)])
            bank_tok[bk] = [tk]
            t_r.append(tk)
        misc["h1_free"] = t_r[-1]
        misc["y_free"] = [last_o]

        wst = {}

        def p_attn_stats():
            t_dresid = None
            if debug and idx == 0:
                t_dresid = dma("gpsimd", dbg["d_resid_0"],
                               resid.rearrange("p c t -> p (c t)"),
                               waits=[t_r[-1]])
            parA, t_mbf2, t_sbc2 = emit_ln_stats(resid, [t_r[-1]],
                                                 chunk_toks=t_r)
            wst["parA"], wst["t_mbf2"], wst["t_sbc2"] = parA, t_mbf2, t_sbc2
            wst["t_dresid"] = t_dresid

        def p_front():
            nb_ = nidx // NT
            if nb_ != b:
                misc["carry0"] = vec(lambda e: e.memset(carry, 0.0),
                                     waits=[t_y[IC - 1]])
                misc["xin0"] = vec(lambda e: e.memset(x_inner[:, :, 0:4], 0.0),
                                   waits=[t_ccar_cur])
            wst["nfront"] = emit_front(nidx)
            wst["ns1p"], wst["ns1out"] = emit_s1_pieces(wst["nfront"])

        def p_attn_apply():
            parA, t_mbf2, t_sbc2 = wst["parA"], wst["t_mbf2"], wst["t_sbc2"]
            t_dresid = wst["t_dresid"]
            mb3 = m_bf[parA].unsqueeze(1).broadcast_to([P, DC, T])
            t_p1b = vec(lambda e, mb3=mb3: e.tensor_tensor(
                sc_a, resid, mb3, OP.subtract),
                waits=[t_mbf2, t_h1a, misc["sc_a_free"],
                       (t_dresid if debug and idx == 0 else None)])
            mbf_free[parA] = t_p1b
            t_x1 = []
            t_sdump = None
            if debug and idx == 0:
                t_cps = vec(lambda e, parA=parA: e.tensor_copy(
                    out_sb[:, 0, :], paux[:, parA * T:(parA + 1) * T]),
                    waits=[t_sbc2])
                t_cpm = vec(lambda e, pA=pstat[parA][:, 0:T]: e.tensor_copy(
                    out_sb[:, 1, :], pA), waits=[t_mbf2, t_cps])
                dma("gpsimd", dbg["d_sd"], out_sb[:, 0, :], waits=[t_cpm])
                t_sdump = dma("gpsimd", dbg["d_md"], out_sb[:, 1, :],
                              waits=[t_cpm])
            t_dsca1 = None
            if debug and idx == 0:
                t_dsca1 = dma("gpsimd", dbg["d_sca1"],
                              sc_a.rearrange("p c t -> p (c t)"),
                              waits=[t_p1b])
            tts = []
            for c in range(DC):
                tt = vec(lambda e, c=c, b=b, parA=parA: e.scalar_tensor_tensor(
                    sc_a[:, c, :], sc_a[:, c, :], vv[:, c, b, 1:2],
                    paux[:, parA * T:(parA + 1) * T],
                    OP.mult, OP.mult),
                    waits=[t_p1b, t_sbc2, (t_dsca1 if c == 0 else None),
                           (t_sdump if c == 0 and t_sdump is not None
                            else None)])
                tts.append(tt)
            t_dsca2 = None
            if debug and idx == 0:
                t_dsca2 = dma("gpsimd", dbg["d_sca2"],
                              sc_a.rearrange("p c t -> p (c t)"),
                              waits=[tts[-1]])
            for c in range(DC):
                tk = vec(lambda e, c=c, xt=xt, b=b: e.scalar_tensor_tensor(
                    resid[:, c, :], sc_a[:, c, :], vv[:, c, b, 2:3],
                    xt[:, c, :], OP.add, OP.add),
                    waits=[tts[c], (t_dsca2 if c == 0 else None)])
                t_x1.append(tk)
            pst_free[parA] = pst_free[parA] + [t_p1b]
            sbc_free[parA] = [t_x1[-1]]
            misc["sc_a_free"] = t_x1[-1]
            misc["x_read"][front["par_t"]] = t_x1[-1]
            wst["t_x1"] = t_x1

        def p_ln2():
            t_x1 = wst["t_x1"]
            parC, t_mbf3, t_sbc3 = emit_ln_stats(resid, [t_x1[-1]],
                                                 chunk_toks=t_x1)
            wst["t_h2"] = emit_ln_apply(parC, t_mbf3, t_sbc3, resid, t_x1[-1],
                                       lambda c, b=b: vv[:, c, b, 6:7],
                                       lambda c, b=b: modv(3, c, b),
                                       h2, misc["h2_free"])

        def p_rtA():
            t_h2 = wst["t_h2"]
            last = None
            for k in range(DC):
                w = [t_h2[k]]
                if k == 0:
                    w += bank_tok[2] + [t_w["router"]]
                last = mm(pool[2][0:8, 0:T], router_w[:, k, :], h2[:, k, :],
                          start=(k == 0), stop=(k == DC - 1), waits=w,
                          inc=(1 if k == DC - 1 else 0))
            t_lg = vec(lambda e: e.tensor_scalar(lg_fm, pool[2][0:8, 0:T],
                                                 router_b, None, OP.add),
                       waits=[last, (t_sync if idx == 0 else None)])
            bank_tok[2] = [t_lg]
            wst["t_lg"] = t_lg

        def p_rtB():
            t_lg = wst["t_lg"]
            t_tp = []
            for tc in range(2):
                tk = tpose(prt[:, (T + tc * 8):(T + tc * 8 + 8)],
                           lg_fm[:, tc * P:(tc + 1) * P], ident[0:8, 0:8],
                           waits=[t_lg] + (prt_free if tc == 0 else []) +
                                 [(t_sync if idx == 0 and tc == 0 else None)])
                t_tp.append(tk)
            t_rt = act(lambda e: e.copy(
                rt, prt[:, T:T + 16].rearrange("p (a e) -> p a e", a=2)),
                waits=[t_tp[-1]])
            t_wtm_pre = []
            for tc in range(2):
                tk = vec(lambda e, tc=tc: e.reduce_max(
                    rm[:, tc, 0:1], rt[:, tc, :], axis=AX.X), waits=[t_rt])
                tk = vec(lambda e, tc=tc: e.tensor_scalar(
                    rmask[:, tc, :], rt[:, tc, :], rm[:, tc, 0:1], None,
                    OP.is_lt), waits=[tk])
                tk = vec(lambda e, tc=tc: e.tensor_scalar(
                    rmask[:, tc, :], rmask[:, tc, :], 1e9, -1e9, OP.mult,
                    OP.add), waits=[tk])
                tk = vec(lambda e, tc=tc: e.tensor_tensor(
                    rmask[:, tc, :], rmask[:, tc, :], rt[:, tc, :], OP.add),
                    waits=[tk])
                tk = vec(lambda e, tc=tc: e.reduce_max(
                    rm[:, tc, 1:2], rmask[:, tc, :], axis=AX.X), waits=[tk])
                tk = vec(lambda e, tc=tc: e.tensor_tensor(
                    rm[:, tc, 2:3], rm[:, tc, 1:2], rm[:, tc, 0:1],
                    OP.subtract), waits=[tk])
                t_wtm_pre.append(tk)
            wst["t_rt"], wst["t_wtm_pre"] = t_rt, t_wtm_pre

        def p_rtC():
            t_rt, t_wtm_pre = wst["t_rt"], wst["t_wtm_pre"]
            t_w1s = act(lambda e: e.activation(rm[:, :, 5:6], rm[:, :, 2:3],
                                               ACTF.Sigmoid, scale=-1.0),
                        waits=t_wtm_pre)
            t_w2s = act(lambda e: e.activation(rm[:, :, 6:7], rm[:, :, 2:3],
                                               ACTF.Sigmoid))
            t_wtm = []
            for tc in range(2):
                tk = vec(lambda e, tc=tc: e.tensor_scalar(
                    w_tm[:, tc, :], rt[:, tc, :], rm[:, tc, 0:1], None,
                    OP.is_equal), waits=[t_w1s, t_w2s])
                tk = vec(lambda e, tc=tc: e.tensor_scalar(
                    rmask[:, tc, :], rt[:, tc, :], rm[:, tc, 1:2], None,
                    OP.is_equal), waits=[tk])
                tk = vec(lambda e, tc=tc: e.tensor_scalar(
                    rmask[:, tc, :], rmask[:, tc, :], rm[:, tc, 6:7], None,
                    OP.mult), waits=[tk])
                tk = vec(lambda e, tc=tc: e.scalar_tensor_tensor(
                    w_tm[:, tc, :], w_tm[:, tc, :], rm[:, tc, 5:6],
                    rmask[:, tc, :], OP.mult, OP.add), waits=[tk])
                t_wtm.append(tk)
            wst["t_wtm"] = t_wtm

        def p_rtD():
            t_rt, t_wtm = wst["t_rt"], wst["t_wtm"]
            t_wfm = []
            t_wfm_ev = None
            for tc in range(2):
                tk = tpose(prt[0:8, (T + 32):(T + 32 + P)],
                           w_tm[:, tc, :], ident,
                           waits=[t_wtm[tc], t_wfm_ev])
                t_wfm_ev = vec(lambda e, tc=tc: e.tensor_scalar(
                    w_fm[:, tc * P:(tc + 1) * P],
                    prt[0:8, (T + 32):(T + 32 + P)],
                    2.0, None, OP.mult), waits=[tk])
                t_wfm.append(t_wfm_ev)
            wst["t_wfm"] = t_wfm
            wst["prt_free_mid"] = [t_rt, t_wfm[-1]]

        def p_wexpax():
            t_h2 = wst["t_h2"]
            t_wfm = wst["t_wfm"]
            t_wexp = mm(prt[:, 0:T], eexp, w_fm, start=True, stop=True,
                        waits=[t_wfm[-1]] + wst["prt_free_mid"] +
                              [(t_sync if idx == 0 else None)], inc=1)
            last = None
            for k in range(DC):
                w = [t_h2[k]]
                if k == 0:
                    w += bank_tok[3] + [t_w["lora_a"]]
                last = mm(pool[3][:, 0:T], lora_a[:, k, :], h2[:, k, :],
                          start=(k == 0), stop=(k == DC - 1), waits=w,
                          inc=(1 if k == DC - 1 else 0))
            t_axe = act(lambda e: e.copy(ax_bf, pool[3][:, 0:T]),
                        waits=[last])
            bank_tok[3] = [t_axe]
            t_axs = vec(lambda e: e.tensor_tensor(axs, ax_bf, prt[:, 0:T],
                                                  OP.mult),
                        waits=[t_axe, t_wexp])
            wst["t_axs"] = t_axs
            wst["prt_free_new"] = wst["prt_free_mid"] + [t_axs]

        t_out = []

        def p_ffn(o):
            t_h2, t_axs, t_x1 = wst["t_h2"], wst["t_axs"], wst["t_x1"]
            bk = 2 + (o % 2)
            last = None
            for k in range(DC):
                w = [t_h2[k]]
                if k == 0:
                    w += bank_tok[bk] + [t_w["base_w"], t_w["lora_b"],
                                         t_w["base_b"]]
                last = mm(pool[bk][:, 0:T], base_w[:, k, o * P:(o + 1) * P],
                          h2[:, k, :], start=(k == 0), stop=False, waits=w)
            last = mm(pool[bk][:, 0:T], base_b_row[0:1, o * P:(o + 1) * P],
                      h2ones, start=False, stop=False,
                      waits=[(t_h2o if idx == 0 and o == 0 else None)])
            last = mm(pool[bk][:, 0:T], lora_b[:, o * P:(o + 1) * P], axs,
                      start=False, stop=True, waits=[t_axs], inc=1)
            tk = vec(lambda e, o=o, bk=bk, b=b: e.scalar_tensor_tensor(
                out_sb[:, o, :], pool[bk][:, 0:T], vv[:, o, b, 3:4],
                resid[:, o, :], OP.mult, OP.add),
                waits=[last, t_x1[o],
                       (misc["store"] if o == 0 else None),
                       (t_bT if idx == 0 and o == 0 else None)])
            bank_tok[bk] = [tk]
            t_out.append(tk)
            if o == DC - 1:
                misc["resid_free"] = tk
                misc["h2_free"] = last

        def p_store():
            t_store = dma("scalar", out_tile_ap(b, j), out_sb,
                          waits=[t_out[-1]])
            misc["store"] = t_store

        if nidx < NB * NT:
            p_attn_stats(); p_front()
            ns1 = wst["ns1p"]
            p_attn_apply()
            p_ln2()
            ns1[0](); ns1[1]()
            p_rtA()
            ns1[2](); ns1[3](); ns1[4]()
            p_rtB(); p_rtC(); p_rtD()
            ns1[5](); ns1[6]()
            p_wexpax()
            ns1[7](); ns1[8](); ns1[9]()
            for o in range(DC - 1):
                p_ffn(o)
                ns1[10 + o]()
            p_ffn(DC - 1)
            p_store()
            prt_free = wst["prt_free_new"]
            front = wst["nfront"]
            s1out = wst["ns1out"]
        else:
            p_attn_stats(); p_attn_apply(); p_ln2()
            p_rtA(); p_rtB(); p_rtC(); p_rtD(); p_wexpax()
            for o in range(DC):
                p_ffn(o)
            p_store()
            prt_free = wst["prt_free_new"]

        if debug and idx == 0:
            dma("scalar", dbg["d_mod"], mod_acc, waits=[t_mod])
            dma("scalar", dbg["d_vv"],
                vv.rearrange("p c b v -> p (c b v)"),
                waits=[misc["bprep"][0]])
            dma("gpsimd", dbg["d_x1_0"], resid.rearrange("p c t -> p (c t)"),
                waits=[wst["t_x1"][-1]])
            dma("gpsimd", dbg["d_y_0"], y_bf.rearrange("p c t -> p (c t)"),
                waits=[t_y[-1]])
            dma("gpsimd", dbg["d_h2_0"], h2.rearrange("p c t -> p (c t)"),
                waits=[wst["t_h2"][-1], misc["h2_free"]])
            dma("gpsimd", dbg["d_xc_0"], x_conv.rearrange("p c t -> p (c t)"),
                waits=[t_xconv[-1], t_y[-1]])
            dma("scalar", dbg["d_wtm_0"],
                w_tm.rearrange("p a e -> p (a e)"), waits=[wst["t_wtm"][-1]])

    S.emit("scalar", lambda e: e.nop(), waits=[misc["store"]], inc=0)

    with nc.Block() as block:
        def runner(ename):
            def body(eng):
                waited = {e: 0 for e in ENGS}
                for ws, fn, inc in S.ops[ename]:
                    for (we, wv) in ws:
                        if wv <= waited[we]:
                            continue
                        eng.wait_ge(sems[we], wv)
                        waited[we] = wv
                    inst = fn(eng)
                    if inc:
                        inst.then_inc(sems[ename], inc)
            return body

        block.sync(runner("sync"))
        block.gpsimd(runner("gpsimd"))
        block.vector(runner("vector"))
        block.scalar(runner("scalar"))
        block.tensor(runner("tensor"))

    return nc, st, S


def _chunked(a, c):
    d, e = a.shape
    return np.ascontiguousarray(a.reshape(c, P, e).transpose(1, 0, 2)
                                .reshape(P, -1))


def _vec_chunk(v):
    return np.ascontiguousarray(v.reshape(-1, P).T)


def make_in_maps(L0_attn_x, L0_attn_W_in, L0_attn_conv_w, L0_attn_conv_b,
                 L0_attn_W_x, L0_attn_B_log, L0_attn_W_out, L0_attn_ln_g,
                 L0_attn_ln_b, L0_ffn_base_w, L0_ffn_base_b, L0_ffn_lora_A,
                 L0_ffn_lora_B, L0_ffn_router_w, L0_ffn_router_b, L0_c,
                 L0_wmod):
    f = lambda a: np.asarray(a, dtype=np.float32)
    x = np.ascontiguousarray(f(L0_attn_x).transpose(0, 2, 1))
    w_in = _chunked(f(L0_attn_W_in).T, DC)
    w_out = _chunked(f(L0_attn_W_out).T, IC)
    base_w = _chunked(f(L0_ffn_base_w).T, DC)
    lora_a = _chunked(f(L0_ffn_lora_A).transpose(1, 0, 2).reshape(D, P), DC)
    lora_b = np.ascontiguousarray(f(L0_ffn_lora_B).reshape(P, D))
    w_x_c = _chunked(f(L0_attn_W_x)[16:32, :].T, IC)
    router_wt = _chunked(f(L0_ffn_router_w).T, DC)
    router_b = np.ascontiguousarray(f(L0_ffn_router_b).reshape(8, 1))
    b_logt = np.ascontiguousarray(f(L0_attn_B_log).T)
    conv_w = _chunked(f(L0_attn_conv_w).reshape(DI, 4), IC)
    conv_b = _vec_chunk(f(L0_attn_conv_b))
    ln_g = _vec_chunk(f(L0_attn_ln_g))
    ln_b = _vec_chunk(f(L0_attn_ln_b))
    base_b_row = np.ascontiguousarray(f(L0_ffn_base_b).reshape(1, D))
    c_all = _chunked(f(L0_c).T, DC).reshape(P, DC, 16)
    wmod = np.ascontiguousarray(f(L0_wmod))
    ident = np.eye(P, dtype=np.float32)
    eexp = np.zeros((8, P), dtype=np.float32)
    for e in range(8):
        eexp[e, e * 16:(e + 1) * 16] = 1.0
    eexp = eexp.astype(ml_dtypes.bfloat16)

    in_maps = []
    for core in range(8):
        bsl = slice(2 * core, 2 * core + 2)
        in_maps.append({
            "x": np.ascontiguousarray(x[bsl]),
            "w_in": w_in, "w_out": w_out, "base_w": base_w,
            "lora_a": lora_a, "lora_b": lora_b, "w_x_c": w_x_c,
            "router_wt": router_wt, "router_b": router_b,
            "b_logt": b_logt, "conv_w": conv_w, "conv_b": conv_b,
            "ln_g": ln_g, "ln_b": ln_b, "base_b_row": base_b_row,
            "c_t": np.ascontiguousarray(c_all[:, :, bsl].reshape(P, -1)),
            "wmod": wmod, "ident": ident, "eexp": eexp,
        })
    return in_maps


def kernel(**inputs):
    in_maps = make_in_maps(**inputs)
    nc, stack, _ = build_program()
    try:
        res = run_bass_kernel_spmd(nc, in_maps, core_ids=list(range(8)))
    finally:
        stack.close()
    outs = []
    for core in range(8):
        o = np.asarray(res.results[core]["out"])
        outs.append(o.transpose(0, 2, 1))
    return np.ascontiguousarray(np.concatenate(outs, axis=0),
                                dtype=np.float32)
